# version 46
# baseline (speedup 1.0000x reference)
import numpy as np
import ml_dtypes
from contextlib import ExitStack

import concourse.bass as bass
import concourse.mybir as mybir
from concourse.bass_utils import run_bass_kernel_spmd

F32 = mybir.dt.float32
BF16 = mybir.dt.bfloat16
AF = mybir.ActivationFunctionType
ALU = mybir.AluOpType
AX = mybir.AxisListType

D = 1024
NH = 16
DH = 64
HID = 4096
CC = 8
LN_EPS = 1e-5
RMS_EPS = 1e-6


class Buf:
    __slots__ = ("w", "r")

    def __init__(self):
        self.w = None
        self.r = []


class Prog:
    ENGS = ("T", "V", "S", "G", "Y")

    def __init__(self, nc, es):
        self.nc = nc
        self.rec = {e: [] for e in self.ENGS}
        self.sems = {}
        self.cnt = {}
        self.covered = {}
        for e in self.ENGS:
            self._mksem("c" + e, es)
        self.dmaq = {}
        self.dmacnt = {}
        for q, n in (("Y", 6), ("G", 6), ("S", 6)):
            self.dmaq[q] = ["d%s%d" % (q, i) for i in range(n)]
            self.dmacnt[q] = 0
            for s in self.dmaq[q]:
                self._mksem(s, es)

    def _mksem(self, name, es):
        self.sems[name] = es.enter_context(self.nc.semaphore(name))
        self.cnt[name] = 0

    def _waits(self, e, tags):
        out = []
        for (s, v) in tags:
            k = (e, s)
            if self.covered.get(k, 0) >= v:
                continue
            self.covered[k] = v
            out.append((s, v))
        return out

    def _deptags(self, reads, writes):
        tags = []
        for b in reads:
            if b.w:
                tags.append(b.w)
        for b in writes:
            if b.w:
                tags.append(b.w)
            tags.extend(b.r)
        return tags

    def _post(self, tag, reads, writes):
        for b in reads:
            b.r.append(tag)
        for b in writes:
            b.w = tag
            b.r = []

    def op(self, e, build, reads=(), writes=()):
        w = self._waits(e, self._deptags(reads, writes))
        s = "c" + e
        self.cnt[s] += 1
        self.rec[e].append((w, build, s, self.cnt[s], 1))
        tag = (s, self.cnt[s])
        self._post(tag, reads, writes)
        return tag

    def opg(self, e, builds, reads=(), writes=()):
        w = self._waits(e, self._deptags(reads, writes))
        s = "c" + e
        self.cnt[s] += 1
        v = self.cnt[s]
        n = len(builds)
        for i, bld in enumerate(builds):
            self.rec[e].append((w if i == 0 else [],
                                bld,
                                s if i == n - 1 else None, v, 1))
        tag = (s, v)
        self._post(tag, reads, writes)
        return tag

    def dma(self, q, build, reads=(), writes=()):
        i = self.dmacnt[q]
        self.dmacnt[q] += 1
        s = self.dmaq[q][i % len(self.dmaq[q])]
        w = self._waits(q, self._deptags(reads, writes) + [(s, self.cnt[s])])
        self.cnt[s] += 16
        self.rec[q].append((w, build, s, self.cnt[s], 16))
        tag = (s, self.cnt[s])
        self._post(tag, reads, writes)
        return tag

    def barrier(self, skip_qs=()):
        snap = dict(self.cnt)
        skip = tuple("d" + q for q in skip_qs)
        for e in self.ENGS:
            tags = [("c" + o, snap["c" + o]) for o in self.ENGS if o != e]
            tags += [(s, snap[s]) for s in snap
                     if s.startswith("d") and not s.startswith(skip)]
            w = self._waits(e, [t for t in tags if t[1] > 0])
            if w:
                self.rec[e].append((w, None, None, None, None))

    def emit(self, block):
        sems = self.sems

        def body_for(e):
            items = self.rec[e]

            def body(eng):
                for (w, build, s, v, inc) in items:
                    for (sn, vv) in w:
                        eng.wait_ge(sems[sn], vv)
                    if build is not None:
                        ins = build(eng)
                        if s is not None:
                            ins.then_inc(sems[s], inc)
            return body

        block.tensor(body_for("T"))
        block.vector(body_for("V"))
        block.scalar(body_for("S"))
        block.gpsimd(body_for("G"))
        block.sync(body_for("Y"))


def ap2(t, offset, dims):
    return bass.AP(t, offset, dims)


VERBOSE = [False]


def _vp(nc, what):
    if VERBOSE[0]:
        print("[sbuf] %-14s base=%7d  used_KB=%.1f" % (what, nc.sbuf_base, (nc.sbuf_base - 16481) / 1024.0))


def build_program(NB, T, dbg=False, verbose=False):
    VERBOSE[0] = verbose
    """One core's full-forward program.  NB batch elements of T tokens."""
    nc = bass.Bass("TRN2", target_bir_lowering=False)
    NTI = T // 128
    NKT = T // 128
    SW = min(1024, T)
    NSW = T // SW
    G = 256
    NG = T // G
    NQG = T // 512

    xd = nc.dram_tensor("x", [NB, T, D], F32, kind="ExternalInput")
    wqkv = nc.dram_tensor("wqkv", [D, 3 * D], F32, kind="ExternalInput")
    bqkv = nc.dram_tensor("bqkv", [3 * D], F32, kind="ExternalInput")
    bqkvc = nc.dram_tensor("bqkvc", [128, 16], F32, kind="ExternalInput")
    wproj = nc.dram_tensor("wproj", [D, D], F32, kind="ExternalInput")
    bproj = nc.dram_tensor("bproj", [D], F32, kind="ExternalInput")
    qn = nc.dram_tensor("qn", [DH, 1], F32, kind="ExternalInput")
    kn = nc.dram_tensor("kn", [DH, 1], F32, kind="ExternalInput")
    w1t = nc.dram_tensor("w1t", [D, HID], F32, kind="ExternalInput")
    b1c = nc.dram_tensor("b1c", [128, 32], F32, kind="ExternalInput")
    w2t = nc.dram_tensor("w2t", [HID, D], F32, kind="ExternalInput")
    b2 = nc.dram_tensor("b2", [D], F32, kind="ExternalInput")
    scol = nc.dram_tensor("scol", [128, 1], F32, kind="ExternalInput")
    identd = nc.dram_tensor("identd", [128, 128], BF16, kind="ExternalInput")
    onesbd = nc.dram_tensor("onesbd", [1, 128], BF16, kind="ExternalInput")
    sel2d = nc.dram_tensor("sel2d", [2, 128], BF16, kind="ExternalInput")
    sel128d = nc.dram_tensor("sel128d", [128, 2], BF16, kind="ExternalInput")
    yd = nc.dram_tensor("y", [NB, T, D], F32, kind="ExternalOutput")
    dk = "ExternalOutput" if dbg else "Internal"
    ffnd = nc.dram_tensor("ffnscr", [NB, T, D], F32, kind=dk)
    gbounce = nc.dram_tensor("gbounce", [DH, 1], F32, kind="Internal")
    w1c_d = nc.dram_tensor("w1c_d", [128, 32768], BF16, kind="Internal")
    w2c_d = nc.dram_tensor("w2c_d", [128, 32768], BF16, kind="Internal")
    qkvc_d = nc.dram_tensor("qkvc_d", [128, 24576], BF16, kind="Internal")
    xn_d = nc.dram_tensor("xn_d", [NB, NG, 128, CC * G], BF16, kind=dk)
    q_d = nc.dram_tensor("q_d", [NB, NH, 64, T], BF16, kind=dk)
    k_d = nc.dram_tensor("k_d", [NB, NH, 64, T], BF16, kind=dk)
    v_d = nc.dram_tensor("v_d", [NB, NH, 128, NKT * 65], BF16, kind=dk)
    ao_d = nc.dram_tensor("ao_d", [NB, NH // 2, 128, T], BF16, kind=dk)
    l_d = nc.dram_tensor("l_d", [NB, NH, T], F32, kind=dk)
    dbgs = {}
    dbg_v1 = dbg_v2q = None
    if dbg:
        dbgs["hT"] = nc.dram_tensor("dbg_hT", [128, 32 * G], BF16, kind="ExternalOutput")
        dbgs["mb"] = nc.dram_tensor("dbg_mb", [128, 1], F32, kind="ExternalOutput")
        dbg_v1 = nc.dram_tensor("dbg_v1", [4, HID], BF16, kind="ExternalOutput")
        dbg_v2q = nc.dram_tensor("dbg_v2q", [128, D], BF16, kind="ExternalOutput")

    with ExitStack() as es0:
        p = Prog(nc, es0)
        e0 = es0.enter_context

        ident = e0(nc.sbuf_tensor("ident", [128, 128], BF16))
        onesb = e0(nc.sbuf_tensor("onesb", [1, 128], BF16))

        sel2 = e0(nc.sbuf_tensor("sel2", [2, 128], BF16))
        sel128 = e0(nc.sbuf_tensor("sel128", [128, 2], BF16))
        scolb = e0(nc.sbuf_tensor("scolb", [128, 1], F32))
        b1cb = e0(nc.sbuf_tensor("b1cb", [128, 32], F32))
        bqkvcb = e0(nc.sbuf_tensor("bqkvcb", [128, 16], F32))
        bvrow = e0(nc.sbuf_tensor("bvrow", [1, D], BF16))
        b2row = e0(nc.sbuf_tensor("b2row", [1, D], BF16))
        bprow = e0(nc.sbuf_tensor("bprow", [1, D], BF16))
        qns = e0(nc.sbuf_tensor("qns", [DH, 1], F32))
        kns = e0(nc.sbuf_tensor("kns", [DH, 1], F32))
        gcol = e0(nc.sbuf_tensor("gcol", [DH, 1], F32))
        gtil = e0(nc.sbuf_tensor("gtil", [128, 1], F32))
        gabs = e0(nc.sbuf_tensor("gabs", [DH, 1], F32))
        grow = e0(nc.sbuf_tensor("grow", [1, DH], F32))
        gmax = e0(nc.sbuf_tensor("gmax", [1, 1], F32))
        mneg = e0(nc.sbuf_tensor("mneg", [1, 1], BF16))
        mbias = e0(nc.sbuf_tensor("mbias", [128, 1], F32))
        epsb = e0(nc.sbuf_tensor("epsb", [128, 2], F32))

        cB = Buf()
        gB = Buf()
        mB = Buf()

        for (dst, src) in ((ident, identd), (onesb, onesbd),
                           (sel2, sel2d), (sel128, sel128d), (scolb, scol),
                           (b1cb, b1c), (bqkvcb, bqkvc), (qns, qn), (kns, kn)):
            p.dma("S", (lambda d_, s_: lambda eng: eng.dma_start(out=d_[:, :], in_=s_[:, :]))(dst, src),
                  writes=(cB,))
        p.dma("G", lambda eng: eng.dma_start(
            out=bvrow[:, :], in_=ap2(bqkv, 2048, [[D, 1], [1, D]])), writes=(cB,))
        p.dma("G", lambda eng: eng.dma_start(
            out=b2row[:, :], in_=ap2(b2, 0, [[D, 1], [1, D]])), writes=(cB,))
        p.dma("G", lambda eng: eng.dma_start(
            out=bprow[:, :], in_=ap2(bproj, 0, [[D, 1], [1, D]])), writes=(cB,))

        p.op("V", lambda eng: eng.memset(epsb[:, 0:1], LN_EPS), writes=(cB,))
        p.op("V", lambda eng: eng.memset(epsb[:, 1:2], RMS_EPS), writes=(cB,))

        p.op("V", lambda eng: eng.tensor_mul(gcol[:, :], qns[:, :], kns[:, :]),
             reads=(cB,), writes=(gB,))
        p.op("V", lambda eng: eng.tensor_copy(gtil[0:DH, 0:1], gcol[:, :]),
             reads=(gB,), writes=(gB,))
        p.op("V", lambda eng: eng.tensor_copy(gtil[DH:128, 0:1], gcol[:, :]),
             reads=(gB,), writes=(gB,))
        p.op("S", lambda eng: eng.activation(gabs[:, :], gcol[:, :], AF.Abs),
             reads=(gB,), writes=(gB,))
        p.dma("S", lambda eng: eng.dma_start(out=gbounce[:, :], in_=gabs[:, :]),
              reads=(gB,), writes=(gB,))
        p.dma("S", lambda eng: eng.dma_start(
            out=grow[:, :], in_=ap2(gbounce, 0, [[DH, 1], [1, DH]])),
            reads=(gB,), writes=(gB,))
        p.op("V", lambda eng: eng.reduce_max(gmax[:, :], grow[:, :], axis=AX.X),
             reads=(gB,), writes=(gB,))
        p.op("V", lambda eng: eng.tensor_scalar(mneg[:, :], gmax[:, :], -8.0, None, op0=ALU.mult),
             reads=(gB,), writes=(gB,))
        with nc.psum_tensor("psM", [128, 1], F32) as psM:
            psMB = Buf()
            p.op("T", lambda eng: eng.matmul(psM[:, :], onesb[:, :], mneg[:, :],
                                             start=True, stop=True),
                 reads=(cB, gB), writes=(psMB,))
            p.op("V", lambda eng: eng.tensor_copy(mbias[:, :], psM[:, :]),
                 reads=(psMB,), writes=(mB,))
        if dbg:
            p.dma("Y", lambda eng: eng.dma_start(out=dbgs["mb"][:, :], in_=mbias[:, :]),
                  reads=(mB,))


        dr = dict(xd=xd, wqkv=wqkv, wproj=wproj, w1t=w1t, w2t=w2t,
                  yd=yd, ffnd=ffnd, xn_d=xn_d, q_d=q_d, k_d=k_d, v_d=v_d,
                  ao_d=ao_d, l_d=l_d, w1c=w1c_d, w2c=w2c_d, qkvc=qkvc_d,
                  dbg_v1=dbg_v1, dbg_v2q=dbg_v2q)
        cb = dict(ident=ident, onesb=onesb, sel2=sel2,
                  sel128=sel128, scolb=scolb, b1cb=b1cb, bqkvcb=bqkvcb,
                  bvrow=bvrow, b2row=b2row, bprow=bprow, gtil=gtil,
                  mbias=mbias, epsb=epsb)
        bufs = dict(cB=cB, mB=mB, gB=gB)

        eb = ExitStack()
        hand = _alloc_hand(nc, eb, 0, T)
        exW = eb.enter_context(ExitStack())
        w1tb, w1B = _alloc_w1(nc, exW, "_b0")
        ea = eb.enter_context(ExitStack())
        qk = _alloc_qkv(nc, ea, "_b0")
        qls = ExitStack()
        lnu, lnfence, lnxnbB = _ln_units(nc, p, 0, T, dr, cb, bufs, hand, qls, "Y", "Y")
        qlu = _qkv_load_units(nc, p, 0, dr, qk, pace=[lnxnbB[c % 3] for c in range(CC)])
        qvu, qfence = _qkv_units(nc, p, 0, T, dr, cb, bufs, hand, qk, qls)
        qi = 0
        wdone = [False]

        def _qstep():
            qvu[qi]()
            if qi + 1 >= min(24, len(qvu)) and not wdone[0]:
                _emit_w1_load(nc, p, 0, dr, w1tb, w1B)
                _emit_qkv_cache_write(nc, p, dr, qk)
                wdone[0] = True
        INTERLEAVE = False
        for i, u in enumerate(lnu):
            ready = (min(len(qvu), 16 * max(0, (i - 3) // 4))
                     if (INTERLEAVE and i > len(qlu)) else 0)
            while qi < min(ready, qi + 6):
                _qstep()
                qi += 1
            u()
            if 1 <= i <= len(qlu):
                qlu[i - 1]()
        for c in range(min(len(lnu) - 1, len(qlu)), len(qlu)):
            qlu[c]()
        while qi < len(qvu):
            _qstep()
            qi += 1
        qls.close()

        for b in range(NB):
            p.barrier(skip_qs=("G",))
            _v_section(nc, p, b, T, dr, cb, bufs, hand, qk, qfence + lnfence)
            ea.close()
            hand["xstk"].close()
            p.barrier()
            _merged(nc, p, b, T, dr, cb, bufs, hand, w1tb, w1B, dbg, dbgs)
            exW.close()
            eb.close()
            p.barrier()
            if b + 1 < NB:
                eb = ExitStack()
                hand = _alloc_hand(nc, eb, b + 1, T)
                with ExitStack() as ep:
                    pun = _proj_units(nc, p, b, T, dr, cb, bufs, ep, overlap=True)
                    with ExitStack() as ln:
                        lun, lnfence, _xb = _ln_units(nc, p, b + 1, T, dr, cb, bufs,
                                                      hand, ln, "G", "S")
                        li, nl, npu = 1, len(lun), len(pun)
                        for i, u in enumerate(pun):
                            u()
                            if i == 0:
                                lun[0]()
                            want = 1 + ((i + 1) * (nl - 1)) // npu
                            while li < want:
                                lun[li]()
                                li += 1
                        while li < nl:
                            lun[li]()
                            li += 1
                p.barrier()
                exW = eb.enter_context(ExitStack())
                w1tb, w1B = _alloc_w1(nc, exW, "_b%d" % (b + 1))
                ea = eb.enter_context(ExitStack())
                qk = _alloc_qkv(nc, ea, "_b%d" % (b + 1))
                for qlun in _qkv_load_units(nc, p, b + 1, dr, qk):
                    qlun()
                _emit_w1_load(nc, p, b + 1, dr, w1tb, w1B)
                qls = ExitStack()
                qvu, qfence = _qkv_units(nc, p, b + 1, T, dr, cb, bufs, hand, qk, qls)
                for u in qvu:
                    u()
                qls.close()
            else:
                with ExitStack() as ep:
                    for u in _proj_units(nc, p, b, T, dr, cb, bufs, ep,
                                         overlap=False):
                        u()
                p.barrier()

        with nc.allow_low_precision(reason="bf16 matmul operands by design; fp32 accumulation"):
            with nc.Block() as block:
                p.emit(block)
    return nc


def _alloc_hand(nc, stk, b, T):
    NQG = T // 512
    sfx = "_b%d" % b
    hand = dict(sfx=sfx)
    hand["v1"] = stk.enter_context(nc.sbuf_tensor("v1" + sfx, [4, HID], BF16))
    hand["v1B"] = Buf()
    hand["v2q"] = stk.enter_context(nc.sbuf_tensor("v2q" + sfx, [128, D], BF16))
    hand["v2qB"] = Buf()
    hand["oddb"] = stk.enter_context(nc.sbuf_tensor("oddb" + sfx, [128, 8 * CC], BF16))
    hand["oddbB"] = Buf()
    hand["xstk"] = stk.enter_context(ExitStack())
    hand["xnT"] = [hand["xstk"].enter_context(
        nc.sbuf_tensor("xnT%d%s" % (c, sfx), [128, T], BF16)) for c in range(CC)]
    hand["xnTgB"] = [Buf() for _ in range(NQG)]
    return hand


def _alloc_w1(nc, stk, sfx):
    w1tb = stk.enter_context(nc.sbuf_tensor("w1tb" + sfx, [128, 32768], BF16, side="right"))
    return w1tb, Buf()


def _emit_w1_load(nc, p, b, dr, w1tb, w1B):
    if b == 0:
        for c in range(CC):
            p.dma("G", (lambda c_: lambda eng: eng.dma_start(
                out=w1tb[:, c_ * HID:(c_ + 1) * HID],
                in_=dr["w1t"][c_ * 128:(c_ + 1) * 128, :]))(c), writes=(w1B,))
    else:
        for c in range(0, CC, 2):
            p.dma("G", (lambda c_: lambda eng: eng.dma_start(
                out=w1tb[:, c_ * HID:(c_ + 2) * HID],
                in_=dr["w1c"][:, c_ * HID:(c_ + 2) * HID]))(c), writes=(w1B,))


def _alloc_qkv(nc, stk, sfx):
    qk = {}
    qk["qkvb"] = stk.enter_context(nc.sbuf_tensor("qkvb" + sfx, [128, 24576], BF16))
    qk["qkvbB"] = Buf()
    return qk


def _qkv_load_units(nc, p, b, dr, qk, pace=None):
    qkvb, qkvbB = qk["qkvb"], qk["qkvbB"]
    units = []
    if b == 0:
        for c in range(CC):
            units.append((lambda c_: lambda: p.dma("G", (lambda cc: lambda eng: eng.dma_start(
                out=qkvb[:, cc * 3072:(cc + 1) * 3072],
                in_=dr["wqkv"][cc * 128:(cc + 1) * 128, :]))(c_),
                reads=(pace[c_],) if pace else (),
                writes=(qkvbB,)))(c))
    else:
        for c in range(0, CC, 2):
            units.append((lambda c_: lambda: p.dma("G", (lambda cc: lambda eng: eng.dma_start(
                out=qkvb[:, cc * 3072:(cc + 2) * 3072],
                in_=dr["qkvc"][:, cc * 3072:(cc + 2) * 3072]))(c_),
                writes=(qkvbB,)))(c))
    return units


def _emit_qkv_cache_write(nc, p, dr, qk):
    qkvb, qkvbB = qk["qkvb"], qk["qkvbB"]
    for c in range(0, CC, 2):
        p.dma("G", (lambda c_: lambda eng: eng.dma_start(
            out=dr["qkvc"][:, c_ * 3072:(c_ + 2) * 3072],
            in_=qkvb[:, c_ * 3072:(c_ + 2) * 3072]))(c), reads=(qkvbB,))


def _ln_units(nc, p, b, T, dr, cb, bufs, hand, lnstk, xq, wq):
    NTI = T // 128
    NG = T // 256
    G = 256
    cB = bufs["cB"]
    sfx = hand["sfx"]
    xnT, xnTgB = hand["xnT"], hand["xnTgB"]
    v1, v1B = hand["v1"], hand["v1B"]
    v2q, v2qB = hand["v2q"], hand["v2qB"]
    oddb, oddbB = hand["oddb"], hand["oddbB"]
    eae = lnstk.enter_context
    xt = [eae(nc.sbuf_tensor("xt%d%s" % (i, sfx), [128, D], F32)) for i in range(3)]
    xtB = [Buf() for _ in range(3)]
    xnb = [eae(nc.sbuf_tensor("xnb%d%s" % (i, sfx), [128, D], BF16)) for i in range(3)]
    xnbB = [Buf() for _ in range(3)]
    xsq = [eae(nc.sbuf_tensor("xsq%d%s" % (i, sfx), [128, D], BF16)) for i in range(1)]
    xsqB = [Buf() for _ in range(1)]
    st1 = [eae(nc.sbuf_tensor("st1%d%s" % (i, sfx), [128, 8], F32)) for i in range(3)]
    st1B = [Buf() for _ in range(3)]
    l04 = eae(nc.sbuf_tensor("l04" + sfx, [4, D], BF16))
    l04B = Buf()
    lq4 = eae(nc.sbuf_tensor("lq4" + sfx, [4, D], BF16))
    lq4B = Buf()
    psTP = [eae(nc.psum_tensor("psTP%d%s" % (i, sfx), [128, 1024], BF16)) for i in range(2)]
    psTPB = [Buf() for _ in range(2)]
    _vp(nc, "phA1" + sfx)

    def _sl(t):
        ist = t % 3
        return dict(
            s1a=st1[ist][:, 0:1], s2a=st1[ist][:, 1:2],
            mea=st1[ist][:, 2:3], m2a=st1[ist][:, 3:4],
            vaa=st1[ist][:, 4:5], sda=st1[ist][:, 5:6],
            rsa=st1[ist][:, 6:7], nma=st1[ist][:, 7:8],
            sB=st1B[ist], xB=xtB[t % 3], nB=xnbB[t % 3], qB=xsqB[0])

    def stA(t):
        L = _sl(t)
        p.dma(xq, (lambda t_, i_: lambda eng: eng.dma_start(
            out=xt[i_][:, :], in_=dr["xd"][b, t_ * 128:(t_ + 1) * 128, :]))(t, t % 3),
            writes=(L["xB"],))
        p.op("S", (lambda i_, a: lambda eng: eng.activation(
            xsq[0][:, :], xt[i_][:, :], AF.Identity, accum_out=a))(t % 3, L["s1a"]),
            reads=(L["xB"],), writes=(L["qB"], L["sB"]))
        p.op("S", (lambda i_, a: lambda eng: eng.activation(
            xsq[0][:, :], xt[i_][:, :], AF.Square, accum_out=a))(t % 3, L["s2a"]),
            reads=(L["xB"], L["qB"]), writes=(L["qB"], L["sB"]))

    def stB(t):
        L = _sl(t)
        p.op("V", (lambda a, c: lambda eng: eng.tensor_scalar(
            a, c, 1.0 / D, None, op0=ALU.mult))(L["mea"], L["s1a"]),
            reads=(L["sB"],), writes=(L["sB"],))
        p.op("V", (lambda a, c: lambda eng: eng.tensor_mul(a, c, c))(L["m2a"], L["mea"]),
             reads=(L["sB"],), writes=(L["sB"],))
        p.op("V", (lambda a, c, d2: lambda eng: eng.tensor_scalar(
            a, c, 1.0 / D, d2, op0=ALU.mult, op1=ALU.subtract))(L["vaa"], L["s2a"], L["m2a"]),
            reads=(L["sB"],), writes=(L["sB"],))

    def stC(t):
        L = _sl(t)
        p.op("S", (lambda a, c: lambda eng: eng.activation(
            a, c, AF.Ln, bias=cb["epsb"][:, 0:1], scale=1.0))(L["sda"], L["vaa"]),
            reads=(L["sB"], cB), writes=(L["sB"],))
        p.op("S", (lambda a, c: lambda eng: eng.activation(
            a, c, AF.Exp, scale=-0.5))(L["rsa"], L["sda"]),
            reads=(L["sB"],), writes=(L["sB"],))

    def stD(t):
        L = _sl(t)
        p.op("V", (lambda a, c, d2: lambda eng: eng.tensor_scalar(
            a, c, d2, -1.0, op0=ALU.mult, op1=ALU.mult))(L["nma"], L["mea"], L["rsa"]),
            reads=(L["sB"],), writes=(L["sB"],))
        p.op("S", (lambda i_, j_, r_, n_: lambda eng: eng.activation(
            xnb[j_][:, :], xt[i_][:, :], AF.Identity, bias=n_, scale=r_))(
                t % 3, t % 3, L["rsa"], L["nma"]),
            reads=(L["xB"], L["sB"]), writes=(L["nB"],))

    def stE(t):
        L = _sl(t)
        if t == 0:
            p.op("V", lambda eng: eng.memset(v1[:, :], 0.0), writes=(v1B,))
            p.op("V", lambda eng: eng.memset(v2q[:, :], 0.0), writes=(v2qB,))
            for r in range(4):
                p.dma("S", (lambda r_, j_: lambda eng: eng.dma_start(
                    out=v1[r_:r_ + 1, r_ * D:(r_ + 1) * D],
                    in_=xnb[j_][1 + 2 * r_:2 + 2 * r_, :]))(r, t % 3),
                    reads=(L["nB"],), writes=(v1B,))
                p.dma("S", (lambda r_, j_: lambda eng: eng.dma_start(
                    out=l04[r_:r_ + 1, :],
                    in_=xnb[j_][9 + 2 * r_:10 + 2 * r_, :]))(r, t % 3),
                    reads=(L["nB"],), writes=(l04B,))
            for q in range(4):
                p.op("V", (lambda q_: lambda eng: eng.tensor_copy(
                    lq4[0:4, q_ * 256:(q_ + 1) * 256],
                    ap2(l04, q_, [[D, 4], [4, 256]])))(q),
                    reads=(l04B,), writes=(lq4B,))
            for q in range(4):
                for r in range(4):
                    p.dma("S", (lambda q_, r_: lambda eng: eng.dma_start(
                        out=v2q[32 * q_ + r_:32 * q_ + r_ + 1, 256 * r_:256 * (r_ + 1)],
                        in_=lq4[r_:r_ + 1, 256 * q_:256 * (q_ + 1)]))(q, r),
                        reads=(lq4B,), writes=(v2qB,))
            if dr.get("dbg_v1") is not None:
                p.dma("S", lambda eng: eng.dma_start(out=dr["dbg_v1"][:, :], in_=v1[:, :]),
                      reads=(v1B,))
                p.dma("S", lambda eng: eng.dma_start(out=dr["dbg_v2q"][:, :], in_=v2q[:, :]),
                      reads=(v2qB,))
        p.opg("T", [(lambda j_, c_, k_: lambda eng: eng.transpose(
            psTP[k_][:, c_ * 128:(c_ + 1) * 128],
            xnb[j_][:, c_ * 128:(c_ + 1) * 128], cb["ident"][:, :]))(t % 3, c, t % 2)
            for c in range(CC)],
            reads=(L["nB"], cB), writes=(psTPB[t % 2],))

    def stF(t):
        for c in range(CC):
            p.op("V", (lambda c_, t_, k_: lambda eng: eng.tensor_copy(
                xnT[c_][:, t_ * 128:(t_ + 1) * 128],
                psTP[k_][:, c_ * 128:(c_ + 1) * 128]))(c, t, t % 2),
                reads=(psTPB[t % 2],), writes=(xnTgB[t // 4],))

    units = []
    for t in range(NTI + 3):
        def step(t=t):
            if t < NTI:
                stA(t)
                stB(t)
            if 1 <= t <= NTI:
                stC(t - 1)
                stD(t - 1)
            if 2 <= t <= NTI + 1:
                stE(t - 2)
            if 3 <= t <= NTI + 2:
                stF(t - 3)
        units.append(step)

    def fin():
        for c in range(CC):
            p.op("V", (lambda c_: lambda eng: eng.tensor_copy(
                oddb[:, c_ * 8:c_ * 8 + 4], xnT[c_][:, 2:10:2]))(c),
                reads=(xnTgB[0],), writes=(oddbB,))
            p.op("V", (lambda c_: lambda eng: eng.tensor_copy(
                oddb[:, c_ * 8 + 4:c_ * 8 + 8], xnT[c_][:, 10:18:2]))(c),
                reads=(xnTgB[0],), writes=(oddbB,))
        for c in range(CC):
            p.dma(wq, (lambda c_: lambda eng: eng.dma_start(
                out=ap2(dr["xn_d"], (b * NG) * 128 * CC * G + c_ * G,
                        [[CC * G, 128], [128 * CC * G, NG], [1, G]]),
                in_=ap2(xnT[c_], 0, [[T, 128], [G, NG], [1, G]])))(c),
                reads=tuple(xnTgB))
    units.append(fin)
    lnfence = tuple(xtB + xnbB + xsqB + st1B + [l04B, lq4B] + psTPB)
    return units, lnfence, xnbB


def _qkv_units(nc, p, b, T, dr, cb, bufs, hand, qk, qstk):
    NKT = T // 128
    NQG = T // 512
    cB, gB = bufs["cB"], bufs["gB"]
    sfx = hand["sfx"]
    xnT, xnTgB = hand["xnT"], hand["xnTgB"]
    qkvb, qkvbB = qk["qkvb"], qk["qkvbB"]
    e3ie = qstk.enter_context
    qkraw = [e3ie(nc.sbuf_tensor("qkraw%d%s" % (i, sfx), [128, 512], F32)) for i in range(3)]
    qkrawB = [Buf() for _ in range(3)]
    qksq = [e3ie(nc.sbuf_tensor("qksq%d%s" % (i, sfx), [128, 512], BF16)) for i in range(2)]
    qksqB = [Buf() for _ in range(2)]
    sstd = [e3ie(nc.sbuf_tensor("sstd%d%s" % (i, sfx), [2, 512], F32)) for i in range(2)]
    sstdB = [Buf() for _ in range(2)]
    rinv = [e3ie(nc.sbuf_tensor("rinv%d%s" % (i, sfx), [2, 512], BF16)) for i in range(3)]
    rinvB = [Buf() for _ in range(3)]
    qkst = [e3ie(nc.sbuf_tensor("qkst%d%s" % (i, sfx), [128, 512], BF16)) for i in range(3)]
    qkstB = [Buf() for _ in range(3)]
    psQK = [e3ie(nc.psum_tensor("psQK%d%s" % (i, sfx), [128, 512], F32)) for i in range(2)]
    psQKB = [Buf() for _ in range(2)]
    psSQ = [e3ie(nc.psum_tensor("psSQ%d%s" % (i, sfx), [2, 512], F32)) for i in range(2)]
    psSQB = [Buf() for _ in range(2)]
    psBC = [e3ie(nc.psum_tensor("psBC%d%s" % (i, sfx), [128, 512], F32)) for i in range(2)]
    psBCB = [Buf() for _ in range(2)]
    _vp(nc, "phA3" + sfx)

    units = [(g, fb) for g in range(NQG) for fb in range(16)]
    NU = len(units)

    def mainmm(u):
        g, fb = units[u]
        i2 = u % 2
        p.opg("T", [(lambda c_, f_, g_, i_: lambda eng: eng.matmul(
            psQK[i_][:, :], qkvb[:, c_ * 3072 + f_ * 128:c_ * 3072 + (f_ + 1) * 128],
            xnT[c_][:, g_ * 512:(g_ + 1) * 512],
            start=(c_ == 0), stop=(c_ == CC - 1)))(c, fb, g, i2)
            for c in range(CC)],
            reads=(qkvbB, xnTgB[g]), writes=(psQKB[i2],))
        p.op("S", (lambda f_, i_, i3: lambda eng: eng.activation(
            qkraw[i3][:, :], psQK[i_][:, :], AF.Identity,
            bias=cb["bqkvcb"][:, f_:f_ + 1], scale=1.0))(fb, i2, u % 3),
            reads=(psQKB[i2], cB), writes=(qkrawB[u % 3],))
        p.op("V", (lambda i3, j_: lambda eng: eng.tensor_mul(
            qksq[j_][:, :], qkraw[i3][:, :], qkraw[i3][:, :]))(u % 3, u % 2),
            reads=(qkrawB[u % 3],), writes=(qksqB[u % 2],))

    def ssqmm(u):
        j = u % 2
        p.op("T", (lambda j_: lambda eng: eng.matmul(
            psSQ[j_][:, :], cb["sel128"][:, :], qksq[j_][:, :],
            start=True, stop=True))(j),
            reads=(qksqB[j], cB), writes=(psSQB[j],))
        p.op("S", (lambda j_: lambda eng: eng.activation(
            sstd[j_][:, :], psSQ[j_][:, :], AF.Ln,
            bias=cb["epsb"][0:2, 1:2], scale=1.0 / DH))(j),
            reads=(psSQB[j], cB), writes=(sstdB[j],))
        p.op("S", (lambda j_, k_: lambda eng: eng.activation(
            rinv[k_][:, :], sstd[j_][:, :], AF.Exp, scale=-0.5))(j, u % 3),
            reads=(sstdB[j],), writes=(rinvB[u % 3],))

    def bcmm(u):
        g, fb = units[u]
        i3, k3, j2, s3 = u % 3, u % 3, u % 2, u % 3
        p.op("T", (lambda k_, j_: lambda eng: eng.matmul(
            psBC[j_][:, :], cb["sel2"][:, :], rinv[k_][:, :],
            start=True, stop=True))(k3, j2),
            reads=(rinvB[k3], cB), writes=(psBCB[j2],))
        if fb < 8:
            p.op("V", (lambda i_, j_, s_: lambda eng: eng.tensor_tensor(
                qkst[s_][:, :], qkraw[i_][:, :], psBC[j_][:, :], op=ALU.mult))(i3, j2, s3),
                reads=(qkrawB[i3], psBCB[j2]), writes=(qkstB[s3],))
            dst = dr["q_d"]
        else:
            p.op("V", (lambda i_, j_, s_: lambda eng: eng.scalar_tensor_tensor(
                qkst[s_][:, :], qkraw[i_][:, :], cb["gtil"][:, 0:1], psBC[j_][:, :],
                op0=ALU.mult, op1=ALU.mult))(i3, j2, s3),
                reads=(qkrawB[i3], psBCB[j2], gB), writes=(qkstB[s3],))
            dst = dr["k_d"]
        h0 = 2 * (fb % 8)
        p.dma("Y", (lambda s_, h_, d_, g_: lambda eng: eng.dma_start(
            out=d_[b, h_, :, g_ * 512:(g_ + 1) * 512], in_=qkst[s_][0:64, :]))(s3, h0, dst, g),
            reads=(qkstB[s3],))
        p.dma("S", (lambda s_, h_, d_, g_: lambda eng: eng.dma_start(
            out=d_[b, h_ + 1, :, g_ * 512:(g_ + 1) * 512], in_=qkst[s_][64:128, :]))(s3, h0, dst, g),
            reads=(qkstB[s3],))

    qunits = []
    for u in range(NU + 2):
        def qu(u=u):
            if u < NU:
                mainmm(u)
            if u >= 1 and u - 1 < NU:
                ssqmm(u - 1)
            if u >= 2 and u - 2 < NU:
                bcmm(u - 2)
        qunits.append(qu)
    fence = tuple(qkrawB + qksqB + sstdB + rinvB + qkstB + psQKB + psSQB + psBCB)
    return qunits, fence


def _v_section(nc, p, b, T, dr, cb, bufs, hand, qk, fence):
    NKT = T // 128
    cB = bufs["cB"]
    sfx = hand["sfx"]
    xnT, xnTgB = hand["xnT"], hand["xnTgB"]
    qkvb, qkvbB = qk["qkvb"], qk["qkvbB"]
    with ExitStack() as e3v:
        vstg = [e3v.enter_context(nc.sbuf_tensor("vstg%d%s" % (i, sfx), [128, 16 * 520], BF16))
                for i in range(2)]
        vstgB = [[Buf() for _ in range(8)] for _ in range(2)]
        psV = [e3v.enter_context(nc.psum_tensor("psV%d%s" % (i, sfx), [128, D], F32)) for i in range(2)]
        psVB = [Buf() for _ in range(2)]
        _vp(nc, "phAv" + sfx)
        for w in range(2):
            p.op("V", (lambda w_: lambda eng: eng.memset(
                ap2(vstg[w_], 64, [[16 * 520, 128], [65, 128]]), 1.0))(w),
                writes=tuple(vstgB[w]) + fence)
        for qt in range(NKT):
            i2 = qt % 2
            w = (qt // 8) % 2
            sl = qt % 8
            blds = []
            for n in (0, 512):
                for c in range(CC):
                    blds.append((lambda c_, n_, q_, i_: lambda eng: eng.matmul(
                        psV[i_][:, n_:n_ + 512], xnT[c_][:, q_ * 128:(q_ + 1) * 128],
                        qkvb[:, c_ * 3072 + 2048 + n_:c_ * 3072 + 2048 + n_ + 512],
                        start=(c_ == 0), stop=False))(c, n, qt, i2))
                blds.append((lambda n_, i_: lambda eng: eng.matmul(
                    psV[i_][:, n_:n_ + 512], cb["onesb"][:, :],
                    cb["bvrow"][0:1, n_:n_ + 512],
                    start=False, stop=True))(n, i2))
            p.opg("T", blds, reads=(xnTgB[qt // 4], qkvbB, cB),
                  writes=(psVB[i2],) + (fence if qt < 2 else ()))
            p.op("S", (lambda i_, w_, s_: lambda eng: eng.copy(
                ap2(vstg[w_], s_ * 65, [[16 * 520, 128], [520, 16], [1, 64]]),
                psV[i_][:, :]))(i2, w, sl),
                reads=(psVB[i2],), writes=(vstgB[w][sl],))
            if sl == 7 or qt == NKT - 1:
                nsl = sl + 1
                kt0 = qt - (nsl - 1)
                for h in range(NH):
                    p.dma("Y" if h % 2 == 0 else "S",
                          (lambda h_, k0, ns, w_: lambda eng: eng.dma_start(
                              out=ap2(dr["v_d"],
                                      ((b * NH + h_) * 128) * (NKT * 65) + k0 * 65,
                                      [[NKT * 65, 128], [1, ns * 65]]),
                              in_=ap2(vstg[w_], h_ * 520,
                                      [[16 * 520, 128], [1, ns * 65]])))(h, kt0, nsl, w),
                          reads=tuple(vstgB[w][0:nsl]))


def _merged(nc, p, b, T, dr, cb, bufs, hand, w1tb, w1B, dbg, dbgs):
    NKT = T // 128
    SW = min(1024, T)
    NSW = T // SW
    G = 256
    NG = T // G
    cB, mB = bufs["cB"], bufs["mB"]
    sfx = hand["sfx"]
    v1, v1B = hand["v1"], hand["v1B"]
    v2q, v2qB = hand["v2q"], hand["v2qB"]
    oddb, oddbB = hand["oddb"], hand["oddbB"]
    exM = ExitStack()
    w2tb = exM.enter_context(nc.sbuf_tensor("w2tb" + sfx, [128, 32768], BF16, side="right"))
    w2B = Buf()
    if b == 0:
        for c in range(0, CC, 2):
            p.dma("G", (lambda c_: lambda eng: eng.dma_start(
                out=dr["w1c"][:, c_ * HID:(c_ + 2) * HID],
                in_=w1tb[:, c_ * HID:(c_ + 2) * HID]))(c), reads=(w1B,))
    with ExitStack() as em:
        eme = em.enter_context
        xng = [eme(nc.sbuf_tensor("xng%d%s" % (i, sfx), [128, CC * G], BF16)) for i in range(2)]
        xngB = [Buf() for _ in range(2)]
        hT = eme(nc.sbuf_tensor("hT" + sfx, [128, 32 * G], BF16))
        hTgB = [Buf() for _ in range(4)]
        s1s = eme(nc.sbuf_tensor("s1s" + sfx, [4, G], BF16))
        s1sB = Buf()
        ttsq = eme(nc.sbuf_tensor("ttsq" + sfx, [128, G], BF16))
        ttsqB = Buf()
        p.op("V", lambda eng: eng.memset(ttsq[:, :], 0.0), writes=(ttsqB,))
        fstb = eme(nc.sbuf_tensor("fstb" + sfx, [128, D], F32))
        fstbB = Buf()
        qth = [eme(nc.sbuf_tensor("qth%d%s" % (i, sfx), [64, T], BF16)) for i in range(2)]
        qthB = [Buf() for _ in range(2)]
        kth = [eme(nc.sbuf_tensor("kth%d%s" % (i, sfx), [64, T], BF16)) for i in range(2)]
        kthB = [Buf() for _ in range(2)]
        vh = [eme(nc.sbuf_tensor("vh%d%s" % (i, sfx), [128, NKT * 65], BF16)) for i in range(2)]
        vhB = [Buf() for _ in range(2)]
        PT = [eme(nc.sbuf_tensor("PT%d%s" % (i, sfx), [128, SW], BF16)) for i in range(3)]
        PTB = [Buf() for _ in range(3)]
        aost = eme(nc.sbuf_tensor("aost" + sfx, [64, SW], BF16))
        aostB = Buf()
        lst = eme(nc.sbuf_tensor("lst" + sfx, [1, SW], F32))
        lstB = Buf()
        psA = eme(nc.psum_tensor("psA" + sfx, [128, G], F32))
        psAB = Buf()
        psF2 = eme(nc.psum_tensor("psF2" + sfx, [128, 512], F32))
        psF2B = Buf()
        psS = [eme(nc.psum_tensor("psSc%d%s" % (i, sfx), [128, SW], F32)) for i in range(2)]
        psSB = [Buf() for _ in range(2)]
        psPV = eme(nc.psum_tensor("psPV" + sfx, [65, SW], F32))
        psPVB = Buf()
        _vp(nc, "merged" + sfx)

        ffn_units = []

        def ffn_group(g):
            gs = g * G
            us = []
            if g + 1 < NG:
                us.append((lambda g_: lambda: p.dma("G", (lambda gg: lambda eng: eng.dma_start(
                    out=xng[gg % 2][:, :],
                    in_=ap2(dr["xn_d"], ((b * NG + gg) * 128) * (CC * G),
                            [[CC * G, 128], [1, CC * G]])))(g_),
                    writes=(xngB[(g_) % 2],)))(g + 1))
            us.append((lambda g_: lambda: (
                p.opg("T", [(lambda c_, i_: lambda eng: eng.matmul(
                    psA[0:4, :], oddb[:, c_ * 8:c_ * 8 + 4],
                    xng[i_][:, c_ * G:(c_ + 1) * G],
                    start=(c_ == 0), stop=(c_ == CC - 1)))(c, g_ % 2)
                    for c in range(CC)],
                    reads=(xngB[g_ % 2], oddbB), writes=(psAB,)),
                p.op("V", lambda eng: eng.tensor_scalar(
                    s1s[:, :], psA[0:4, :], cb["scolb"][0:4, 0:1], None, op0=ALU.mult),
                    reads=(psAB, cB), writes=(s1sB,))))(g))
            for fb in range(32):
                def u_f1(f_, g_):
                    def run():
                        blds = [(lambda c_, ff, i_: lambda eng: eng.matmul(
                            psA[:, :], w1tb[:, c_ * HID + ff * 128:c_ * HID + (ff + 1) * 128],
                            xng[i_][:, c_ * G:(c_ + 1) * G],
                            start=(c_ == 0), stop=False))(c, f_, g_ % 2)
                            for c in range(CC)]
                        blds.append((lambda ff: lambda eng: eng.matmul(
                            psA[:, :], v1[:, ff * 128:(ff + 1) * 128], s1s[:, :],
                            start=False, stop=True))(f_))
                        p.opg("T", blds, reads=(w1B, xngB[g_ % 2], v1B, s1sB), writes=(psAB,))
                        p.op("S", (lambda ff: lambda eng: eng.activation(
                            hT[:, ff * G:(ff + 1) * G], psA[:, :], AF.Identity,
                            bias=cb["b1cb"][:, ff:ff + 1], scale=1.0))(f_),
                            reads=(psAB, cB), writes=(hTgB[f_ // 8],))
                    return run
                us.append(u_f1(fb, g))
            tail = []

            def u_gelu():
                for bb in range(4):
                    p.op("S", (lambda b_: lambda eng: eng.activation(
                        hT[:, b_ * 8 * G:(b_ + 1) * 8 * G],
                        hT[:, b_ * 8 * G:(b_ + 1) * 8 * G], AF.Gelu))(bb),
                        reads=(hTgB[bb],), writes=(hTgB[bb],))
            tail.append(u_gelu)
            for bp in range(4):
                def u_t(b_, g_):
                    def run():
                        p.opg("T", [(lambda bb, j_: lambda eng: eng.matmul(
                            psA[0:4, :], oddb[:, j_ * 8 + 4:j_ * 8 + 8],
                            hT[:, (bb * 8 + j_) * G:(bb * 8 + j_ + 1) * G],
                            start=(j_ == 0), stop=(j_ == CC - 1)))(b_, jc)
                            for jc in range(CC)],
                            reads=(oddbB, hTgB[b_]), writes=(psAB,))
                        p.op("V", (lambda bb: lambda eng: eng.tensor_scalar(
                            ttsq[32 * bb:32 * bb + 4, :], psA[0:4, :], cb["scolb"][0:4, 0:1],
                            None, op0=ALU.mult))(b_),
                            reads=(psAB, cB), writes=(ttsqB,))
                    return run
                tail.append(u_t(bp, g))
            for qtl in range(G // 128):
                for n in (0, 512):
                    def u_f2(q_, n_, g_):
                        def run():
                            qoff = q_ * 128
                            blds = []
                            for hc in range(32):
                                blds.append((lambda h_: lambda eng: eng.matmul(
                                    psF2[:, :],
                                    hT[:, h_ * G + qoff:h_ * G + qoff + 128],
                                    w2tb[:, h_ * D + n_:h_ * D + n_ + 512],
                                    start=(h_ == 0), stop=False))(hc))
                            blds.append(lambda eng: eng.matmul(
                                psF2[:, :], ttsq[:, qoff:qoff + 128],
                                v2q[:, n_:n_ + 512], start=False, stop=False))
                            blds.append(lambda eng: eng.matmul(
                                psF2[:, :], cb["onesb"][:, :],
                                cb["b2row"][0:1, n_:n_ + 512],
                                start=False, stop=True))
                            p.opg("T", blds,
                                  reads=tuple(hTgB) + (w2B, ttsqB, cB, v2qB),
                                  writes=(psF2B,))
                            p.op("V", lambda eng: eng.tensor_copy(
                                fstb[:, n_:n_ + 512], psF2[:, :]),
                                reads=(psF2B,), writes=(fstbB,))
                            if n_ == 512:
                                p.dma("S", (lambda t0: lambda eng: eng.dma_start(
                                    out=dr["ffnd"][b, t0:t0 + 128, :], in_=fstb[:, :]))(g_ * G + qoff),
                                    reads=(fstbB,))
                        return run
                    tail.append(u_f2(qtl, n, g))
            return us, tail

        pending_tail = []
        for g in range(NG):
            us, tail = ffn_group(g)
            ffn_units.extend(pending_tail)
            ffn_units.extend(us)
            pending_tail = tail
        ffn_units.extend(pending_tail)
        total_f = len(ffn_units)
        total_kt = NH * NSW * NKT
        state = {"fe": 0, "ke": 0}

        def pump():
            while (state["fe"] < total_f and
                   state["fe"] * total_kt <= (state["ke"] + 8) * total_f):
                ffn_units[state["fe"]]()
                state["fe"] += 1

        def stream_head(h_):
            sl = h_ % 2
            p.dma("Y", (lambda hh, s_: lambda eng: eng.dma_start(
                out=kth[s_][:, :], in_=dr["k_d"][b, hh, :, :]))(h_, sl),
                writes=(kthB[sl],))
            p.dma("Y", (lambda hh, s_: lambda eng: eng.dma_start(
                out=qth[s_][:, :], in_=dr["q_d"][b, hh, :, :]))(h_, sl),
                writes=(qthB[sl],))
            p.dma("S", (lambda hh, s_: lambda eng: eng.dma_start(
                out=vh[s_][:, :],
                in_=ap2(dr["v_d"], ((b * NH + hh) * 128) * (NKT * 65),
                        [[NKT * 65, 128], [1, NKT * 65]])))(h_, sl),
                writes=(vhB[sl],))

        p.dma("G", lambda eng: eng.dma_start(
            out=xng[0][:, :],
            in_=ap2(dr["xn_d"], (b * NG) * 128 * (CC * G),
                    [[CC * G, 128], [1, CC * G]])), writes=(xngB[0],))
        stream_head(0)
        stream_head(1)
        if b == 0:
            for hq in range(8):
                p.dma("G", (lambda h_: lambda eng: eng.dma_start(
                    out=w2tb[:, h_ * 4 * D:(h_ + 1) * 4 * D],
                    in_=ap2(dr["w2t"], h_ * 512 * D,
                            [[D, 128], [128 * D, 4], [1, D]])))(hq), writes=(w2B,))
        else:
            for hq in range(0, 8, 2):
                p.dma("G", (lambda h_: lambda eng: eng.dma_start(
                    out=w2tb[:, h_ * 4096:(h_ + 2) * 4096],
                    in_=dr["w2c"][:, h_ * 4096:(h_ + 2) * 4096]))(hq), writes=(w2B,))
        if b == 0:
            for hq in range(0, 8, 2):
                p.dma("G", (lambda h_: lambda eng: eng.dma_start(
                    out=dr["w2c"][:, h_ * 4096:(h_ + 2) * 4096],
                    in_=w2tb[:, h_ * 4096:(h_ + 2) * 4096]))(hq), reads=(w2B,))

        for hh in range(NH):
            sl = hh % 2
            for sw in range(NSW):
                q0 = sw * SW

                def pv(kt):
                    i3 = kt % 3
                    edge = kt == 0 or kt == NKT - 1
                    p.opg("T", [(lambda k_, n_, i_, s_: lambda eng: eng.matmul(
                        psPV[:, n_:n_ + 512],
                        vh[s_][:, k_ * 65:(k_ + 1) * 65],
                        PT[i_][:, n_:n_ + 512],
                        start=(k_ == 0), stop=(k_ == NKT - 1)))(kt, n, i3, sl)
                        for n in range(0, SW, 512)],
                        reads=(PTB[i3], vhB[sl]),
                        writes=(psPVB,) if edge else ())

                for kt in range(NKT):
                    i2 = kt % 2
                    p.opg("T", [(lambda k_, n_, i_, s_, q_: lambda eng: eng.matmul(
                        psS[i_][:, n_:n_ + 512],
                        kth[s_][:, k_ * 128:(k_ + 1) * 128],
                        qth[s_][:, q_ + n_:q_ + n_ + 512],
                        start=True, stop=True))(kt, n, i2, sl, q0)
                        for n in range(0, SW, 512)],
                        reads=(kthB[sl], qthB[sl]), writes=(psSB[i2],))
                    p.op("S", (lambda i_, i3_: lambda eng: eng.activation(
                        PT[i3_][:, :], psS[i_][:, :], AF.Exp,
                        bias=cb["mbias"][:, 0:1], scale=0.125))(i2, kt % 3),
                        reads=(psSB[i2], mB), writes=(PTB[kt % 3],))
                    if kt > 1:
                        pv(kt - 2)
                    state["ke"] += 1
                    pump()
                pv(NKT - 2)
                pv(NKT - 1)
                p.op("V", lambda eng: eng.tensor_copy(aost[:, :], psPV[0:64, :]),
                     reads=(psPVB,), writes=(aostB,))
                p.op("V", lambda eng: eng.tensor_copy(lst[:, :], psPV[64:65, :]),
                     reads=(psPVB,), writes=(lstB,))
                p.dma("Y", (lambda hh_, q_: lambda eng: eng.dma_start(
                    out=dr["ao_d"][b, hh_ >> 1, 64 * (hh_ & 1):64 * (hh_ & 1) + 64, q_:q_ + SW],
                    in_=aost[:, :]))(hh, q0),
                    reads=(aostB,))
                p.dma("S", (lambda hh_, q_: lambda eng: eng.dma_start(
                    out=dr["l_d"][b, hh_, q_:q_ + SW], in_=lst[:, :]))(hh, q0),
                    reads=(lstB,))
            if hh + 2 < NH:
                stream_head(hh + 2)
        while state["fe"] < total_f:
            ffn_units[state["fe"]]()
            state["fe"] += 1
        if dbg:
            p.dma("Y", lambda eng: eng.dma_start(out=dbgs["hT"][:, :], in_=hT[:, :]),
                  reads=tuple(hTgB))
    exM.close()


def _proj_units(nc, p, b, T, dr, cb, bufs, ep, overlap=False):
    NTI = T // 128
    SW = min(1024, T)
    NP = NH // 2
    cB = bufs["cB"]
    sfx = "_p%d" % b
    epe = ep.enter_context
    wprojb = epe(nc.sbuf_tensor("wprojb" + sfx, [128, 8192], BF16))
    wpB = Buf()
    aoTn = epe(nc.sbuf_tensor("aoTn" + sfx, [128, 8 * T], BF16))
    aoTnB = Buf()
    aorw = [epe(nc.sbuf_tensor("aorw%d%s" % (i, sfx), [128, T], BF16)) for i in range(2)]
    aorwB = [Buf() for _ in range(2)]
    lpr = epe(nc.sbuf_tensor("lpr" + sfx, [NH, T], F32))
    lprB = Buf()
    linv16 = epe(nc.sbuf_tensor("linv16" + sfx, [NH, T], BF16))
    linv16B = Buf()
    leo = [epe(nc.sbuf_tensor("leo%d%s" % (i, sfx), [1, 2 * T], BF16)) for i in range(2)]
    leoB = [Buf() for _ in range(2)]
    xres = [epe(nc.sbuf_tensor("xres%d%s" % (i, sfx), [128, D], F32)) for i in range(3)]
    xresB = [Buf() for _ in range(3)]
    fres = [epe(nc.sbuf_tensor("fres%d%s" % (i, sfx), [128, D], F32)) for i in range(3)]
    fresB = [Buf() for _ in range(3)]
    yt = [epe(nc.sbuf_tensor("yt%d%s" % (i, sfx), [128, D], F32)) for i in range(2)]
    ytB = [Buf() for _ in range(2)]
    psBC2 = [epe(nc.psum_tensor("psBC2%d%s" % (i, sfx), [128, SW], F32)) for i in range(2)]
    psBC2B = [Buf() for _ in range(2)]
    psPJ = [epe(nc.psum_tensor("psPJ%d%s" % (i, sfx), [128, 512 if overlap else D], F32))
            for i in range(2)]
    psPJB = [Buf() for _ in range(2)]
    _vp(nc, "proj" + sfx)

    def fetch_res(q_):
        i3_ = q_ % 3
        xq = "S" if q_ % 2 == 1 else "Y"
        fq = "S" if (overlap or q_ % 2 == 0) else "G"
        p.dma(xq, (lambda qq, ii: lambda eng: eng.dma_start(
            out=xres[ii][:, :], in_=dr["xd"][b, qq * 128:(qq + 1) * 128, :]))(q_, i3_),
            writes=(xresB[i3_],))
        p.dma(fq, (lambda qq, ii: lambda eng: eng.dma_start(
            out=fres[ii][:, :], in_=dr["ffnd"][b, qq * 128:(qq + 1) * 128, :]))(q_, i3_),
            writes=(fresB[i3_],))

    units = []

    def wpload():
        for c in range(CC):
            p.dma("G", (lambda c_: lambda eng: eng.dma_start(
                out=wprojb[:, c_ * D:(c_ + 1) * D],
                in_=dr["wproj"][c_ * 128:(c_ + 1) * 128, :]))(c), writes=(wpB,))

    def u0():
        if not overlap:
            wpload()
        for q_ in range(min(3, NTI)):
            fetch_res(q_)
        p.dma("S", lambda eng: eng.dma_start(out=lpr[:, :], in_=dr["l_d"][b, :, :]),
              writes=(lprB,))
        p.op("S", lambda eng: eng.activation(lpr[:, :], lpr[:, :], AF.Ln),
             reads=(lprB,), writes=(lprB,))
        p.op("S", lambda eng: eng.activation(linv16[:, :], lpr[:, :], AF.Exp, scale=-1.0),
             reads=(lprB,), writes=(linv16B,))
    units.append(u0)

    def u_pg(pg):
        i2 = pg % 2
        p.dma("Y", (lambda p_, i_: lambda eng: eng.dma_start(
            out=aorw[i_][:, :],
            in_=ap2(dr["ao_d"], ((b * NP + p_) * 128) * T,
                    [[T, 128], [1, T]])))(pg, i2),
            writes=(aorwB[i2],))
        for r in range(2):
            p.dma("S", (lambda p_, i_, r_: lambda eng: eng.dma_start(
                out=leo[i_][:, r_ * T:(r_ + 1) * T],
                in_=linv16[2 * p_ + r_:2 * p_ + r_ + 1, :]))(pg, i2, r),
                reads=(linv16B,), writes=(leoB[i2],))
        for q0 in range(0, T, SW):
            p.opg("T", [(lambda hf, n_, q_, i_, j_: lambda eng: eng.matmul(
                psBC2[j_][hf:hf + 64, n_:n_ + 512], cb["onesb"][0:1, 0:64],
                leo[i_][0:1, (hf // 64) * T + q_ + n_:(hf // 64) * T + q_ + n_ + 512],
                start=True, stop=True))(half, n, q0, i2, pg % 2)
                for half in (0, 64) for n in range(0, SW, 512)],
                reads=(leoB[i2], cB), writes=(psBC2B[pg % 2],))
            p.op("V", (lambda p_, i_, q_, j_: lambda eng: eng.tensor_tensor(
                aoTn[:, p_ * T + q_:p_ * T + q_ + SW],
                aorw[i_][:, q_:q_ + SW],
                psBC2[j_][:, :], op=ALU.mult))(pg, i2, q0, pg % 2),
                reads=(psBC2B[pg % 2], aorwB[i2]), writes=(aoTnB,))
    for pg in range(NP):
        units.append((lambda p_: lambda: u_pg(p_))(pg))
        if overlap and pg == 2:
            units.append(wpload)

    def u_qt(qt):
        i2 = qt % 2
        i3r = qt % 3
        p.op("V", (lambda i_, j_: lambda eng: eng.tensor_add(
            yt[i_][:, :], xres[j_][:, :], fres[j_][:, :]))(i2, i3r),
            reads=(xresB[i3r], fresB[i3r]), writes=(ytB[i2],))
        if overlap:
            for ih, n in enumerate((0, 512)):
                blds = []
                for c in range(CC):
                    blds.append((lambda c_, n_, q_, i_: lambda eng: eng.matmul(
                        psPJ[i_][:, :], aoTn[:, c_ * T + q_ * 128:c_ * T + (q_ + 1) * 128],
                        wprojb[:, c_ * D + n_:c_ * D + n_ + 512],
                        start=(c_ == 0), stop=False))(c, n, qt, ih))
                blds.append((lambda n_, i_: lambda eng: eng.matmul(
                    psPJ[i_][:, :], cb["onesb"][:, :], cb["bprow"][0:1, n_:n_ + 512],
                    start=False, stop=True))(n, ih))
                p.opg("T", blds, reads=(wpB, cB, aoTnB), writes=(psPJB[ih],))
                p.op("V", (lambda i_, n_, j_: lambda eng: eng.tensor_add(
                    yt[i_][:, n_:n_ + 512], yt[i_][:, n_:n_ + 512], psPJ[j_][:, :]))(i2, n, ih),
                    reads=(psPJB[ih], ytB[i2]), writes=(ytB[i2],))
        else:
            blds = []
            for n in (0, 512):
                for c in range(CC):
                    blds.append((lambda c_, n_, q_, i_: lambda eng: eng.matmul(
                        psPJ[i_][:, n_:n_ + 512], aoTn[:, c_ * T + q_ * 128:c_ * T + (q_ + 1) * 128],
                        wprojb[:, c_ * D + n_:c_ * D + n_ + 512],
                        start=(c_ == 0), stop=False))(c, n, qt, i2))
                blds.append((lambda n_, i_: lambda eng: eng.matmul(
                    psPJ[i_][:, n_:n_ + 512], cb["onesb"][:, :], cb["bprow"][0:1, n_:n_ + 512],
                    start=False, stop=True))(n, i2))
            p.opg("T", blds, reads=(wpB, cB, aoTnB), writes=(psPJB[i2],))
            p.op("V", (lambda i_: lambda eng: eng.tensor_add(
                yt[i_][:, :], yt[i_][:, :], psPJ[i_][:, :]))(i2),
                reads=(psPJB[i2], ytB[i2]), writes=(ytB[i2],))
        p.dma("Y" if qt % 2 == 0 else "S", (lambda q_, i_: lambda eng: eng.dma_start(
            out=dr["yd"][b, q_ * 128:(q_ + 1) * 128, :], in_=yt[i_][:, :]))(qt, i2),
            reads=(ytB[i2],))
        if qt + 3 < NTI:
            fetch_res(qt + 3)
    for qt in range(NTI):
        units.append((lambda q_: lambda: u_qt(q_))(qt))
    return units


def _host_consts():
    bf = ml_dtypes.bfloat16
    ident = np.eye(128, dtype=bf)
    onesb = np.ones((1, 128), dtype=bf)
    sel2 = np.zeros((2, 128), dtype=bf)
    sel2[0, 0:64] = 1
    sel2[1, 64:128] = 1
    sel128 = np.zeros((128, 2), dtype=bf)
    sel128[0:64, 0] = 1
    sel128[64:128, 1] = 1
    return ident, onesb, sel2, sel128


def make_in_maps(inputs, NB, T, n_cores):
    x = np.ascontiguousarray(np.asarray(inputs["L0_attn_x"], dtype=np.float32))
    wqkv = np.ascontiguousarray(np.asarray(inputs["L0_attn_W_qkv"], dtype=np.float32))
    bqkv = np.ascontiguousarray(np.asarray(inputs["L0_attn_b_qkv"], dtype=np.float32))
    wproj = np.ascontiguousarray(np.asarray(inputs["L0_attn_W_proj"], dtype=np.float32))
    bproj = np.ascontiguousarray(np.asarray(inputs["L0_attn_b_proj"], dtype=np.float32))
    qn = np.ascontiguousarray(np.asarray(inputs["L0_attn_qn_w"], dtype=np.float32).reshape(DH, 1))
    kn = np.ascontiguousarray(np.asarray(inputs["L0_attn_kn_w"], dtype=np.float32).reshape(DH, 1))
    w1 = np.asarray(inputs["L0_ffn_W1"], dtype=np.float32)
    b1 = np.asarray(inputs["L0_ffn_b1"], dtype=np.float32)
    w2 = np.asarray(inputs["L0_ffn_W2"], dtype=np.float32)
    b2 = np.ascontiguousarray(np.asarray(inputs["L0_ffn_b2"], dtype=np.float32))
    scale = np.asarray(inputs["L0_ffn_scale"], dtype=np.float32)

    w1t = np.ascontiguousarray(w1.T)
    w2t = np.ascontiguousarray(w2.T)
    b1c = np.ascontiguousarray(b1.reshape(32, 128).T)
    bqkvc = np.ascontiguousarray(bqkv[:2048].reshape(16, 128).T)
    scol = np.ascontiguousarray(np.broadcast_to(scale.reshape(1, 1), (128, 1)))
    ident, onesb, sel2, sel128 = _host_consts()

    xs = x.reshape(n_cores, NB, T, D)
    common = dict(wqkv=wqkv, bqkv=bqkv, bqkvc=bqkvc, wproj=wproj, bproj=bproj,
                  qn=qn, kn=kn, w1t=w1t, b1c=b1c, w2t=w2t, b2=b2, scol=scol,
                  identd=ident, onesbd=onesb, sel2d=sel2,
                  sel128d=sel128)
    return [dict(common, x=np.ascontiguousarray(xs[i])) for i in range(n_cores)]


def kernel(**inputs):
    NB, T, NCORES = 2, 2048, 8
    nc = build_program(NB, T, dbg=False)
    in_maps = make_in_maps(inputs, NB, T, NCORES)
    res = run_bass_kernel_spmd(nc, in_maps, core_ids=list(range(NCORES)))
    y = np.concatenate([np.asarray(r["y"]) for r in res.results], axis=0)
    return np.ascontiguousarray(y.reshape(16, 2048, 1024)).astype(np.float32)


# version 47
# speedup vs baseline: 1.0510x; 1.0510x over previous
import numpy as np
import ml_dtypes
from contextlib import ExitStack

import concourse.bass as bass
import concourse.mybir as mybir
from concourse.bass_utils import run_bass_kernel_spmd

F32 = mybir.dt.float32
BF16 = mybir.dt.bfloat16
AF = mybir.ActivationFunctionType
ALU = mybir.AluOpType
AX = mybir.AxisListType

D = 1024
NH = 16
DH = 64
HID = 4096
CC = 8
LN_EPS = 1e-5
RMS_EPS = 1e-6


class Buf:
    __slots__ = ("w", "r")

    def __init__(self):
        self.w = None
        self.r = []


class Prog:
    ENGS = ("T", "V", "S", "G", "Y")

    def __init__(self, nc, es):
        self.nc = nc
        self.rec = {e: [] for e in self.ENGS}
        self.sems = {}
        self.cnt = {}
        self.covered = {}
        for e in self.ENGS:
            self._mksem("c" + e, es)
        self.dmaq = {}
        self.dmacnt = {}
        for q, n in (("Y", 6), ("G", 6), ("S", 6)):
            self.dmaq[q] = ["d%s%d" % (q, i) for i in range(n)]
            self.dmacnt[q] = 0
            for s in self.dmaq[q]:
                self._mksem(s, es)

    def _mksem(self, name, es):
        self.sems[name] = es.enter_context(self.nc.semaphore(name))
        self.cnt[name] = 0

    def _waits(self, e, tags):
        out = []
        for (s, v) in tags:
            k = (e, s)
            if self.covered.get(k, 0) >= v:
                continue
            self.covered[k] = v
            out.append((s, v))
        return out

    def _deptags(self, reads, writes):
        tags = []
        for b in reads:
            if b.w:
                tags.append(b.w)
        for b in writes:
            if b.w:
                tags.append(b.w)
            tags.extend(b.r)
        return tags

    def _post(self, tag, reads, writes):
        for b in reads:
            b.r.append(tag)
        for b in writes:
            b.w = tag
            b.r = []

    def op(self, e, build, reads=(), writes=()):
        w = self._waits(e, self._deptags(reads, writes))
        s = "c" + e
        self.cnt[s] += 1
        self.rec[e].append((w, build, s, self.cnt[s], 1))
        tag = (s, self.cnt[s])
        self._post(tag, reads, writes)
        return tag

    def opg(self, e, builds, reads=(), writes=()):
        w = self._waits(e, self._deptags(reads, writes))
        s = "c" + e
        self.cnt[s] += 1
        v = self.cnt[s]
        n = len(builds)
        for i, bld in enumerate(builds):
            self.rec[e].append((w if i == 0 else [],
                                bld,
                                s if i == n - 1 else None, v, 1))
        tag = (s, v)
        self._post(tag, reads, writes)
        return tag

    def dma(self, q, build, reads=(), writes=()):
        i = self.dmacnt[q]
        self.dmacnt[q] += 1
        s = self.dmaq[q][i % len(self.dmaq[q])]
        w = self._waits(q, self._deptags(reads, writes) + [(s, self.cnt[s])])
        self.cnt[s] += 16
        self.rec[q].append((w, build, s, self.cnt[s], 16))
        tag = (s, self.cnt[s])
        self._post(tag, reads, writes)
        return tag

    def barrier(self, skip_qs=()):
        snap = dict(self.cnt)
        skip = tuple("d" + q for q in skip_qs)
        for e in self.ENGS:
            tags = [("c" + o, snap["c" + o]) for o in self.ENGS if o != e]
            tags += [(s, snap[s]) for s in snap
                     if s.startswith("d") and not s.startswith(skip)]
            w = self._waits(e, [t for t in tags if t[1] > 0])
            if w:
                self.rec[e].append((w, None, None, None, None))

    def emit(self, block):
        sems = self.sems

        def body_for(e):
            items = self.rec[e]

            def body(eng):
                for (w, build, s, v, inc) in items:
                    for (sn, vv) in w:
                        eng.wait_ge(sems[sn], vv)
                    if build is not None:
                        ins = build(eng)
                        if s is not None:
                            ins.then_inc(sems[s], inc)
            return body

        block.tensor(body_for("T"))
        block.vector(body_for("V"))
        block.scalar(body_for("S"))
        block.gpsimd(body_for("G"))
        block.sync(body_for("Y"))


def ap2(t, offset, dims):
    return bass.AP(t, offset, dims)


VERBOSE = [False]


def _vp(nc, what):
    if VERBOSE[0]:
        print("[sbuf] %-14s base=%7d  used_KB=%.1f" % (what, nc.sbuf_base, (nc.sbuf_base - 16481) / 1024.0))


def build_program(NB, T, dbg=False, verbose=False):
    VERBOSE[0] = verbose
    """One core's full-forward program.  NB batch elements of T tokens."""
    nc = bass.Bass("TRN2", target_bir_lowering=False)
    NTI = T // 128
    NKT = T // 128
    SW = min(1024, T)
    NSW = T // SW
    G = 256
    NG = T // G
    NQG = T // 512

    xd = nc.dram_tensor("x", [NB, T, D], F32, kind="ExternalInput")
    wqkv = nc.dram_tensor("wqkv", [D, 3 * D], F32, kind="ExternalInput")
    bqkv = nc.dram_tensor("bqkv", [3 * D], F32, kind="ExternalInput")
    bqkvc = nc.dram_tensor("bqkvc", [128, 16], F32, kind="ExternalInput")
    wproj = nc.dram_tensor("wproj", [D, D], F32, kind="ExternalInput")
    bproj = nc.dram_tensor("bproj", [D], F32, kind="ExternalInput")
    qn = nc.dram_tensor("qn", [DH, 1], F32, kind="ExternalInput")
    kn = nc.dram_tensor("kn", [DH, 1], F32, kind="ExternalInput")
    w1t = nc.dram_tensor("w1t", [D, HID], F32, kind="ExternalInput")
    b1c = nc.dram_tensor("b1c", [128, 32], F32, kind="ExternalInput")
    w2t = nc.dram_tensor("w2t", [HID, D], F32, kind="ExternalInput")
    b2 = nc.dram_tensor("b2", [D], F32, kind="ExternalInput")
    scol = nc.dram_tensor("scol", [128, 1], F32, kind="ExternalInput")
    identd = nc.dram_tensor("identd", [128, 128], BF16, kind="ExternalInput")
    onesbd = nc.dram_tensor("onesbd", [1, 128], BF16, kind="ExternalInput")
    sel2d = nc.dram_tensor("sel2d", [2, 128], BF16, kind="ExternalInput")
    sel128d = nc.dram_tensor("sel128d", [128, 2], BF16, kind="ExternalInput")
    yd = nc.dram_tensor("y", [NB, T, D], F32, kind="ExternalOutput")
    dk = "ExternalOutput" if dbg else "Internal"
    ffnd = nc.dram_tensor("ffnscr", [NB, T, D], F32, kind=dk)
    gbounce = nc.dram_tensor("gbounce", [DH, 1], F32, kind="Internal")
    w1c_d = nc.dram_tensor("w1c_d", [128, 32768], BF16, kind="Internal")
    w2c_d = nc.dram_tensor("w2c_d", [128, 32768], BF16, kind="Internal")
    qkvc_d = nc.dram_tensor("qkvc_d", [128, 24576], BF16, kind="Internal")
    xn_d = nc.dram_tensor("xn_d", [NB, NG, 128, CC * G], BF16, kind=dk)
    q_d = nc.dram_tensor("q_d", [NB, NH, 64, T], BF16, kind=dk)
    k_d = nc.dram_tensor("k_d", [NB, NH, 64, T], BF16, kind=dk)
    v_d = nc.dram_tensor("v_d", [NB, NH, 128, NKT * 65], BF16, kind=dk)
    ao_d = nc.dram_tensor("ao_d", [NB, NH // 2, 128, T], BF16, kind=dk)
    l_d = nc.dram_tensor("l_d", [NB, NH, T], F32, kind=dk)
    dbgs = {}
    dbg_v1 = dbg_v2q = None
    if dbg:
        dbgs["hT"] = nc.dram_tensor("dbg_hT", [128, 32 * G], BF16, kind="ExternalOutput")
        dbgs["mb"] = nc.dram_tensor("dbg_mb", [128, 1], F32, kind="ExternalOutput")
        dbg_v1 = nc.dram_tensor("dbg_v1", [4, HID], BF16, kind="ExternalOutput")
        dbg_v2q = nc.dram_tensor("dbg_v2q", [128, D], BF16, kind="ExternalOutput")

    with ExitStack() as es0:
        p = Prog(nc, es0)
        e0 = es0.enter_context

        ident = e0(nc.sbuf_tensor("ident", [128, 128], BF16))
        onesb = e0(nc.sbuf_tensor("onesb", [1, 128], BF16))

        sel2 = e0(nc.sbuf_tensor("sel2", [2, 128], BF16))
        sel128 = e0(nc.sbuf_tensor("sel128", [128, 2], BF16))
        scolb = e0(nc.sbuf_tensor("scolb", [128, 1], F32))
        b1cb = e0(nc.sbuf_tensor("b1cb", [128, 32], F32))
        bqkvcb = e0(nc.sbuf_tensor("bqkvcb", [128, 16], F32))
        bvrow = e0(nc.sbuf_tensor("bvrow", [1, D], BF16))
        b2row = e0(nc.sbuf_tensor("b2row", [1, D], BF16))
        bprow = e0(nc.sbuf_tensor("bprow", [1, D], BF16))
        qns = e0(nc.sbuf_tensor("qns", [DH, 1], F32))
        kns = e0(nc.sbuf_tensor("kns", [DH, 1], F32))
        gcol = e0(nc.sbuf_tensor("gcol", [DH, 1], F32))
        gtil = e0(nc.sbuf_tensor("gtil", [128, 1], F32))
        gabs = e0(nc.sbuf_tensor("gabs", [DH, 1], F32))
        grow = e0(nc.sbuf_tensor("grow", [1, DH], F32))
        gmax = e0(nc.sbuf_tensor("gmax", [1, 1], F32))
        mneg = e0(nc.sbuf_tensor("mneg", [1, 1], BF16))
        mbias = e0(nc.sbuf_tensor("mbias", [128, 1], F32))
        epsb = e0(nc.sbuf_tensor("epsb", [128, 2], F32))

        cB = Buf()
        gB = Buf()
        mB = Buf()

        for (dst, src) in ((ident, identd), (onesb, onesbd),
                           (sel2, sel2d), (sel128, sel128d), (scolb, scol),
                           (b1cb, b1c), (bqkvcb, bqkvc), (qns, qn), (kns, kn)):
            p.dma("S", (lambda d_, s_: lambda eng: eng.dma_start(out=d_[:, :], in_=s_[:, :]))(dst, src),
                  writes=(cB,))
        p.dma("G", lambda eng: eng.dma_start(
            out=bvrow[:, :], in_=ap2(bqkv, 2048, [[D, 1], [1, D]])), writes=(cB,))
        p.dma("G", lambda eng: eng.dma_start(
            out=b2row[:, :], in_=ap2(b2, 0, [[D, 1], [1, D]])), writes=(cB,))
        p.dma("G", lambda eng: eng.dma_start(
            out=bprow[:, :], in_=ap2(bproj, 0, [[D, 1], [1, D]])), writes=(cB,))

        p.op("V", lambda eng: eng.memset(epsb[:, 0:1], LN_EPS), writes=(cB,))
        p.op("V", lambda eng: eng.memset(epsb[:, 1:2], RMS_EPS), writes=(cB,))

        p.op("V", lambda eng: eng.tensor_mul(gcol[:, :], qns[:, :], kns[:, :]),
             reads=(cB,), writes=(gB,))
        p.op("V", lambda eng: eng.tensor_copy(gtil[0:DH, 0:1], gcol[:, :]),
             reads=(gB,), writes=(gB,))
        p.op("V", lambda eng: eng.tensor_copy(gtil[DH:128, 0:1], gcol[:, :]),
             reads=(gB,), writes=(gB,))
        p.op("S", lambda eng: eng.activation(gabs[:, :], gcol[:, :], AF.Abs),
             reads=(gB,), writes=(gB,))
        p.dma("S", lambda eng: eng.dma_start(out=gbounce[:, :], in_=gabs[:, :]),
              reads=(gB,), writes=(gB,))
        p.dma("S", lambda eng: eng.dma_start(
            out=grow[:, :], in_=ap2(gbounce, 0, [[DH, 1], [1, DH]])),
            reads=(gB,), writes=(gB,))
        p.op("V", lambda eng: eng.reduce_max(gmax[:, :], grow[:, :], axis=AX.X),
             reads=(gB,), writes=(gB,))
        p.op("V", lambda eng: eng.tensor_scalar(mneg[:, :], gmax[:, :], -8.0, None, op0=ALU.mult),
             reads=(gB,), writes=(gB,))
        with nc.psum_tensor("psM", [128, 1], F32) as psM:
            psMB = Buf()
            p.op("T", lambda eng: eng.matmul(psM[:, :], onesb[:, :], mneg[:, :],
                                             start=True, stop=True),
                 reads=(cB, gB), writes=(psMB,))
            p.op("V", lambda eng: eng.tensor_copy(mbias[:, :], psM[:, :]),
                 reads=(psMB,), writes=(mB,))
        if dbg:
            p.dma("Y", lambda eng: eng.dma_start(out=dbgs["mb"][:, :], in_=mbias[:, :]),
                  reads=(mB,))


        dr = dict(xd=xd, wqkv=wqkv, wproj=wproj, w1t=w1t, w2t=w2t,
                  yd=yd, ffnd=ffnd, xn_d=xn_d, q_d=q_d, k_d=k_d, v_d=v_d,
                  ao_d=ao_d, l_d=l_d, w1c=w1c_d, w2c=w2c_d, qkvc=qkvc_d,
                  dbg_v1=dbg_v1, dbg_v2q=dbg_v2q)
        cb = dict(ident=ident, onesb=onesb, sel2=sel2,
                  sel128=sel128, scolb=scolb, b1cb=b1cb, bqkvcb=bqkvcb,
                  bvrow=bvrow, b2row=b2row, bprow=bprow, gtil=gtil,
                  mbias=mbias, epsb=epsb)
        bufs = dict(cB=cB, mB=mB, gB=gB)

        eb = ExitStack()
        hand = _alloc_hand(nc, eb, 0, T)
        exW = eb.enter_context(ExitStack())
        w1tb, w1B = _alloc_w1(nc, exW, "_b0")
        ea = eb.enter_context(ExitStack())
        qk = _alloc_qkv(nc, ea, "_b0")
        qls = ExitStack()
        lnu, lnfence, lnxnbB = _ln_units(nc, p, 0, T, dr, cb, bufs, hand, qls, "Y", "Y")
        qlu = _qkv_load_units(nc, p, 0, dr, qk, pace=[lnxnbB[c % 3] for c in range(CC)])
        qvu, qfence = _qkv_units(nc, p, 0, T, dr, cb, bufs, hand, qk, qls)
        qi = 0
        wdone = [False]

        def _qstep():
            qvu[qi]()
            if qi + 1 >= min(24, len(qvu)) and not wdone[0]:
                _emit_w1_load(nc, p, 0, dr, w1tb, w1B)
                _emit_qkv_cache_write(nc, p, dr, qk)
                wdone[0] = True
        INTERLEAVE = False
        for i, u in enumerate(lnu):
            ready = (min(len(qvu), 16 * max(0, (i - 3) // 4))
                     if (INTERLEAVE and i > len(qlu)) else 0)
            while qi < min(ready, qi + 6):
                _qstep()
                qi += 1
            u()
            if 1 <= i <= len(qlu):
                qlu[i - 1]()
        for c in range(min(len(lnu) - 1, len(qlu)), len(qlu)):
            qlu[c]()
        while qi < len(qvu):
            _qstep()
            qi += 1
        qls.close()

        for b in range(NB):
            p.barrier(skip_qs=("G",))
            _v_section(nc, p, b, T, dr, cb, bufs, hand, qk, qfence + lnfence)
            ea.close()
            hand["xstk"].close()
            p.barrier()
            _merged(nc, p, b, T, dr, cb, bufs, hand, w1tb, w1B, dbg, dbgs)
            exW.close()
            eb.close()
            p.barrier()
            if b + 1 < NB:
                eb = ExitStack()
                hand = _alloc_hand(nc, eb, b + 1, T)
                with ExitStack() as ep:
                    pun = _proj_units(nc, p, b, T, dr, cb, bufs, ep, overlap=True)
                    with ExitStack() as ln:
                        lun, lnfence, _xb = _ln_units(nc, p, b + 1, T, dr, cb, bufs,
                                                      hand, ln, "G", "S")
                        li, nl, npu = 1, len(lun), len(pun)
                        for i, u in enumerate(pun):
                            u()
                            if i == 0:
                                lun[0]()
                            want = 1 + ((i + 1) * (nl - 1)) // npu
                            while li < want:
                                lun[li]()
                                li += 1
                        while li < nl:
                            lun[li]()
                            li += 1
                p.barrier()
                exW = eb.enter_context(ExitStack())
                w1tb, w1B = _alloc_w1(nc, exW, "_b%d" % (b + 1))
                ea = eb.enter_context(ExitStack())
                qk = _alloc_qkv(nc, ea, "_b%d" % (b + 1))
                for qlun in _qkv_load_units(nc, p, b + 1, dr, qk):
                    qlun()
                _emit_w1_load(nc, p, b + 1, dr, w1tb, w1B)
                qls = ExitStack()
                qvu, qfence = _qkv_units(nc, p, b + 1, T, dr, cb, bufs, hand, qk, qls)
                for u in qvu:
                    u()
                qls.close()
            else:
                with ExitStack() as ep:
                    for u in _proj_units(nc, p, b, T, dr, cb, bufs, ep,
                                         overlap=False):
                        u()
                p.barrier()

        with nc.allow_low_precision(reason="bf16 matmul operands by design; fp32 accumulation"):
            with nc.Block() as block:
                p.emit(block)
    return nc


def _alloc_hand(nc, stk, b, T):
    NQG = T // 512
    sfx = "_b%d" % b
    hand = dict(sfx=sfx)
    hand["v1"] = stk.enter_context(nc.sbuf_tensor("v1" + sfx, [4, HID], BF16))
    hand["v1B"] = Buf()
    hand["v2q"] = stk.enter_context(nc.sbuf_tensor("v2q" + sfx, [128, D], BF16))
    hand["v2qB"] = Buf()
    hand["oddb"] = stk.enter_context(nc.sbuf_tensor("oddb" + sfx, [128, 8 * CC], BF16))
    hand["oddbB"] = Buf()
    hand["xstk"] = stk.enter_context(ExitStack())
    hand["xnT"] = [hand["xstk"].enter_context(
        nc.sbuf_tensor("xnT%d%s" % (c, sfx), [128, T], BF16)) for c in range(CC)]
    hand["xnTgB"] = [Buf() for _ in range(NQG)]
    return hand


def _alloc_w1(nc, stk, sfx):
    w1tb = stk.enter_context(nc.sbuf_tensor("w1tb" + sfx, [128, 32768], BF16, side="right"))
    return w1tb, Buf()


def _emit_w1_load(nc, p, b, dr, w1tb, w1B):
    if b == 0:
        for c in range(CC):
            p.dma("G", (lambda c_: lambda eng: eng.dma_start(
                out=w1tb[:, c_ * HID:(c_ + 1) * HID],
                in_=dr["w1t"][c_ * 128:(c_ + 1) * 128, :]))(c), writes=(w1B,))
    else:
        for c in range(0, CC, 2):
            p.dma("G", (lambda c_: lambda eng: eng.dma_start(
                out=w1tb[:, c_ * HID:(c_ + 2) * HID],
                in_=dr["w1c"][:, c_ * HID:(c_ + 2) * HID]))(c), writes=(w1B,))


def _alloc_qkv(nc, stk, sfx):
    qk = {}
    qk["qkvb"] = stk.enter_context(nc.sbuf_tensor("qkvb" + sfx, [128, 24576], BF16))
    qk["qkvbB"] = Buf()
    return qk


def _qkv_load_units(nc, p, b, dr, qk, pace=None):
    qkvb, qkvbB = qk["qkvb"], qk["qkvbB"]
    units = []
    if b == 0:
        for c in range(CC):
            units.append((lambda c_: lambda: p.dma("G", (lambda cc: lambda eng: eng.dma_start(
                out=qkvb[:, cc * 3072:(cc + 1) * 3072],
                in_=dr["wqkv"][cc * 128:(cc + 1) * 128, :]))(c_),
                reads=(pace[c_],) if pace else (),
                writes=(qkvbB,)))(c))
    else:
        for c in range(0, CC, 2):
            units.append((lambda c_: lambda: p.dma("G", (lambda cc: lambda eng: eng.dma_start(
                out=qkvb[:, cc * 3072:(cc + 2) * 3072],
                in_=dr["qkvc"][:, cc * 3072:(cc + 2) * 3072]))(c_),
                writes=(qkvbB,)))(c))
    return units


def _emit_qkv_cache_write(nc, p, dr, qk):
    qkvb, qkvbB = qk["qkvb"], qk["qkvbB"]
    for c in range(0, CC, 2):
        p.dma("G", (lambda c_: lambda eng: eng.dma_start(
            out=dr["qkvc"][:, c_ * 3072:(c_ + 2) * 3072],
            in_=qkvb[:, c_ * 3072:(c_ + 2) * 3072]))(c), reads=(qkvbB,))


def _ln_units(nc, p, b, T, dr, cb, bufs, hand, lnstk, xq, wq):
    NTI = T // 128
    NG = T // 256
    G = 256
    cB = bufs["cB"]
    sfx = hand["sfx"]
    xnT, xnTgB = hand["xnT"], hand["xnTgB"]
    v1, v1B = hand["v1"], hand["v1B"]
    v2q, v2qB = hand["v2q"], hand["v2qB"]
    oddb, oddbB = hand["oddb"], hand["oddbB"]
    eae = lnstk.enter_context
    xt = [eae(nc.sbuf_tensor("xt%d%s" % (i, sfx), [128, D], F32)) for i in range(3)]
    xtB = [Buf() for _ in range(3)]
    xnb = [eae(nc.sbuf_tensor("xnb%d%s" % (i, sfx), [128, D], BF16)) for i in range(3)]
    xnbB = [Buf() for _ in range(3)]
    xsq = [eae(nc.sbuf_tensor("xsq%d%s" % (i, sfx), [128, D], BF16)) for i in range(1)]
    xsqB = [Buf() for _ in range(1)]
    st1 = [eae(nc.sbuf_tensor("st1%d%s" % (i, sfx), [128, 8], F32)) for i in range(3)]
    st1B = [Buf() for _ in range(3)]
    l04 = eae(nc.sbuf_tensor("l04" + sfx, [4, D], BF16))
    l04B = Buf()
    lq4 = eae(nc.sbuf_tensor("lq4" + sfx, [4, D], BF16))
    lq4B = Buf()
    psTP = [eae(nc.psum_tensor("psTP%d%s" % (i, sfx), [128, 1024], BF16)) for i in range(2)]
    psTPB = [Buf() for _ in range(2)]
    _vp(nc, "phA1" + sfx)

    def _sl(t):
        ist = t % 3
        return dict(
            s1a=st1[ist][:, 0:1], s2a=st1[ist][:, 1:2],
            mea=st1[ist][:, 2:3], m2a=st1[ist][:, 3:4],
            vaa=st1[ist][:, 4:5], sda=st1[ist][:, 5:6],
            rsa=st1[ist][:, 6:7], nma=st1[ist][:, 7:8],
            sB=st1B[ist], xB=xtB[t % 3], nB=xnbB[t % 3], qB=xsqB[0])

    def stA(t):
        L = _sl(t)
        p.dma(xq, (lambda t_, i_: lambda eng: eng.dma_start(
            out=xt[i_][:, :], in_=dr["xd"][b, t_ * 128:(t_ + 1) * 128, :]))(t, t % 3),
            writes=(L["xB"],))
        p.op("S", (lambda i_, a: lambda eng: eng.activation(
            xsq[0][:, :], xt[i_][:, :], AF.Identity, accum_out=a))(t % 3, L["s1a"]),
            reads=(L["xB"],), writes=(L["qB"], L["sB"]))
        p.op("S", (lambda i_, a: lambda eng: eng.activation(
            xsq[0][:, :], xt[i_][:, :], AF.Square, accum_out=a))(t % 3, L["s2a"]),
            reads=(L["xB"], L["qB"]), writes=(L["qB"], L["sB"]))

    def stB(t):
        L = _sl(t)
        p.op("V", (lambda a, c: lambda eng: eng.tensor_scalar(
            a, c, 1.0 / D, None, op0=ALU.mult))(L["mea"], L["s1a"]),
            reads=(L["sB"],), writes=(L["sB"],))
        p.op("V", (lambda a, c: lambda eng: eng.tensor_mul(a, c, c))(L["m2a"], L["mea"]),
             reads=(L["sB"],), writes=(L["sB"],))
        p.op("V", (lambda a, c, d2: lambda eng: eng.tensor_scalar(
            a, c, 1.0 / D, d2, op0=ALU.mult, op1=ALU.subtract))(L["vaa"], L["s2a"], L["m2a"]),
            reads=(L["sB"],), writes=(L["sB"],))

    def stC(t):
        L = _sl(t)
        p.op("S", (lambda a, c: lambda eng: eng.activation(
            a, c, AF.Ln, bias=cb["epsb"][:, 0:1], scale=1.0))(L["sda"], L["vaa"]),
            reads=(L["sB"], cB), writes=(L["sB"],))
        p.op("S", (lambda a, c: lambda eng: eng.activation(
            a, c, AF.Exp, scale=-0.5))(L["rsa"], L["sda"]),
            reads=(L["sB"],), writes=(L["sB"],))

    def stD(t):
        L = _sl(t)
        p.op("V", (lambda a, c, d2: lambda eng: eng.tensor_scalar(
            a, c, d2, -1.0, op0=ALU.mult, op1=ALU.mult))(L["nma"], L["mea"], L["rsa"]),
            reads=(L["sB"],), writes=(L["sB"],))
        p.op("S", (lambda i_, j_, r_, n_: lambda eng: eng.activation(
            xnb[j_][:, :], xt[i_][:, :], AF.Identity, bias=n_, scale=r_))(
                t % 3, t % 3, L["rsa"], L["nma"]),
            reads=(L["xB"], L["sB"]), writes=(L["nB"],))

    def stE(t):
        L = _sl(t)
        if t == 0:
            p.op("V", lambda eng: eng.memset(v1[:, :], 0.0), writes=(v1B,))
            p.op("V", lambda eng: eng.memset(v2q[:, :], 0.0), writes=(v2qB,))
            for r in range(4):
                p.dma("S", (lambda r_, j_: lambda eng: eng.dma_start(
                    out=v1[r_:r_ + 1, r_ * D:(r_ + 1) * D],
                    in_=xnb[j_][1 + 2 * r_:2 + 2 * r_, :]))(r, t % 3),
                    reads=(L["nB"],), writes=(v1B,))
                p.dma("S", (lambda r_, j_: lambda eng: eng.dma_start(
                    out=l04[r_:r_ + 1, :],
                    in_=xnb[j_][9 + 2 * r_:10 + 2 * r_, :]))(r, t % 3),
                    reads=(L["nB"],), writes=(l04B,))
            for q in range(4):
                p.op("V", (lambda q_: lambda eng: eng.tensor_copy(
                    lq4[0:4, q_ * 256:(q_ + 1) * 256],
                    ap2(l04, q_, [[D, 4], [4, 256]])))(q),
                    reads=(l04B,), writes=(lq4B,))
            for q in range(4):
                for r in range(4):
                    p.dma("S", (lambda q_, r_: lambda eng: eng.dma_start(
                        out=v2q[32 * q_ + r_:32 * q_ + r_ + 1, 256 * r_:256 * (r_ + 1)],
                        in_=lq4[r_:r_ + 1, 256 * q_:256 * (q_ + 1)]))(q, r),
                        reads=(lq4B,), writes=(v2qB,))
            if dr.get("dbg_v1") is not None:
                p.dma("S", lambda eng: eng.dma_start(out=dr["dbg_v1"][:, :], in_=v1[:, :]),
                      reads=(v1B,))
                p.dma("S", lambda eng: eng.dma_start(out=dr["dbg_v2q"][:, :], in_=v2q[:, :]),
                      reads=(v2qB,))
        p.opg("T", [(lambda j_, c_, k_: lambda eng: eng.transpose(
            psTP[k_][:, c_ * 128:(c_ + 1) * 128],
            xnb[j_][:, c_ * 128:(c_ + 1) * 128], cb["ident"][:, :]))(t % 3, c, t % 2)
            for c in range(CC)],
            reads=(L["nB"], cB), writes=(psTPB[t % 2],))

    def stF(t):
        for c in range(CC):
            p.op("V", (lambda c_, t_, k_: lambda eng: eng.tensor_copy(
                xnT[c_][:, t_ * 128:(t_ + 1) * 128],
                psTP[k_][:, c_ * 128:(c_ + 1) * 128]))(c, t, t % 2),
                reads=(psTPB[t % 2],), writes=(xnTgB[t // 4],))

    units = []
    for t in range(NTI + 3):
        def step(t=t):
            if t < NTI:
                stA(t)
                stB(t)
            if 1 <= t <= NTI:
                stC(t - 1)
                stD(t - 1)
            if 2 <= t <= NTI + 1:
                stE(t - 2)
            if 3 <= t <= NTI + 2:
                stF(t - 3)
        units.append(step)

    def fin():
        for c in range(CC):
            p.op("V", (lambda c_: lambda eng: eng.tensor_copy(
                oddb[:, c_ * 8:c_ * 8 + 4], xnT[c_][:, 2:10:2]))(c),
                reads=(xnTgB[0],), writes=(oddbB,))
            p.op("V", (lambda c_: lambda eng: eng.tensor_copy(
                oddb[:, c_ * 8 + 4:c_ * 8 + 8], xnT[c_][:, 10:18:2]))(c),
                reads=(xnTgB[0],), writes=(oddbB,))
        for c in range(CC):
            p.dma(wq, (lambda c_: lambda eng: eng.dma_start(
                out=ap2(dr["xn_d"], (b * NG) * 128 * CC * G + c_ * G,
                        [[CC * G, 128], [128 * CC * G, NG], [1, G]]),
                in_=ap2(xnT[c_], 0, [[T, 128], [G, NG], [1, G]])))(c),
                reads=tuple(xnTgB))
    units.append(fin)
    lnfence = tuple(xtB + xnbB + xsqB + st1B + [l04B, lq4B] + psTPB)
    return units, lnfence, xnbB


def _qkv_units(nc, p, b, T, dr, cb, bufs, hand, qk, qstk):
    NKT = T // 128
    NQG = T // 512
    cB, gB = bufs["cB"], bufs["gB"]
    sfx = hand["sfx"]
    xnT, xnTgB = hand["xnT"], hand["xnTgB"]
    qkvb, qkvbB = qk["qkvb"], qk["qkvbB"]
    e3ie = qstk.enter_context
    qkraw = [e3ie(nc.sbuf_tensor("qkraw%d%s" % (i, sfx), [128, 512], F32)) for i in range(3)]
    qkrawB = [Buf() for _ in range(3)]
    qksq = [e3ie(nc.sbuf_tensor("qksq%d%s" % (i, sfx), [128, 512], BF16)) for i in range(2)]
    qksqB = [Buf() for _ in range(2)]
    sstd = [e3ie(nc.sbuf_tensor("sstd%d%s" % (i, sfx), [2, 512], F32)) for i in range(2)]
    sstdB = [Buf() for _ in range(2)]
    rinv = [e3ie(nc.sbuf_tensor("rinv%d%s" % (i, sfx), [2, 512], BF16)) for i in range(3)]
    rinvB = [Buf() for _ in range(3)]
    qkst = [e3ie(nc.sbuf_tensor("qkst%d%s" % (i, sfx), [128, 512], BF16)) for i in range(3)]
    qkstB = [Buf() for _ in range(3)]
    psQK = [e3ie(nc.psum_tensor("psQK%d%s" % (i, sfx), [128, 512], F32)) for i in range(2)]
    psQKB = [Buf() for _ in range(2)]
    psSQ = [e3ie(nc.psum_tensor("psSQ%d%s" % (i, sfx), [2, 512], F32)) for i in range(2)]
    psSQB = [Buf() for _ in range(2)]
    psBC = [e3ie(nc.psum_tensor("psBC%d%s" % (i, sfx), [128, 512], F32)) for i in range(2)]
    psBCB = [Buf() for _ in range(2)]
    _vp(nc, "phA3" + sfx)

    units = [(g, fb) for g in range(NQG) for fb in range(16)]
    NU = len(units)

    def mainmm(u):
        g, fb = units[u]
        i2 = u % 2
        p.opg("T", [(lambda c_, f_, g_, i_: lambda eng: eng.matmul(
            psQK[i_][:, :], qkvb[:, c_ * 3072 + f_ * 128:c_ * 3072 + (f_ + 1) * 128],
            xnT[c_][:, g_ * 512:(g_ + 1) * 512],
            start=(c_ == 0), stop=(c_ == CC - 1)))(c, fb, g, i2)
            for c in range(CC)],
            reads=(qkvbB, xnTgB[g]), writes=(psQKB[i2],))
        p.op("S", (lambda f_, i_, i3: lambda eng: eng.activation(
            qkraw[i3][:, :], psQK[i_][:, :], AF.Identity,
            bias=cb["bqkvcb"][:, f_:f_ + 1], scale=1.0))(fb, i2, u % 3),
            reads=(psQKB[i2], cB), writes=(qkrawB[u % 3],))
        p.op("V", (lambda i3, j_: lambda eng: eng.tensor_mul(
            qksq[j_][:, :], qkraw[i3][:, :], qkraw[i3][:, :]))(u % 3, u % 2),
            reads=(qkrawB[u % 3],), writes=(qksqB[u % 2],))

    def ssqmm(u):
        j = u % 2
        p.op("T", (lambda j_: lambda eng: eng.matmul(
            psSQ[j_][:, :], cb["sel128"][:, :], qksq[j_][:, :],
            start=True, stop=True))(j),
            reads=(qksqB[j], cB), writes=(psSQB[j],))
        p.op("S", (lambda j_: lambda eng: eng.activation(
            sstd[j_][:, :], psSQ[j_][:, :], AF.Ln,
            bias=cb["epsb"][0:2, 1:2], scale=1.0 / DH))(j),
            reads=(psSQB[j], cB), writes=(sstdB[j],))
        p.op("S", (lambda j_, k_: lambda eng: eng.activation(
            rinv[k_][:, :], sstd[j_][:, :], AF.Exp, scale=-0.5))(j, u % 3),
            reads=(sstdB[j],), writes=(rinvB[u % 3],))

    def bcmm(u):
        g, fb = units[u]
        i3, k3, j2, s3 = u % 3, u % 3, u % 2, u % 3
        p.op("T", (lambda k_, j_: lambda eng: eng.matmul(
            psBC[j_][:, :], cb["sel2"][:, :], rinv[k_][:, :],
            start=True, stop=True))(k3, j2),
            reads=(rinvB[k3], cB), writes=(psBCB[j2],))
        if fb < 8:
            p.op("V", (lambda i_, j_, s_: lambda eng: eng.tensor_tensor(
                qkst[s_][:, :], qkraw[i_][:, :], psBC[j_][:, :], op=ALU.mult))(i3, j2, s3),
                reads=(qkrawB[i3], psBCB[j2]), writes=(qkstB[s3],))
            dst = dr["q_d"]
        else:
            p.op("V", (lambda i_, j_, s_: lambda eng: eng.scalar_tensor_tensor(
                qkst[s_][:, :], qkraw[i_][:, :], cb["gtil"][:, 0:1], psBC[j_][:, :],
                op0=ALU.mult, op1=ALU.mult))(i3, j2, s3),
                reads=(qkrawB[i3], psBCB[j2], gB), writes=(qkstB[s3],))
            dst = dr["k_d"]
        h0 = 2 * (fb % 8)
        p.dma("Y", (lambda s_, h_, d_, g_: lambda eng: eng.dma_start(
            out=d_[b, h_, :, g_ * 512:(g_ + 1) * 512], in_=qkst[s_][0:64, :]))(s3, h0, dst, g),
            reads=(qkstB[s3],))
        p.dma("S", (lambda s_, h_, d_, g_: lambda eng: eng.dma_start(
            out=d_[b, h_ + 1, :, g_ * 512:(g_ + 1) * 512], in_=qkst[s_][64:128, :]))(s3, h0, dst, g),
            reads=(qkstB[s3],))

    qunits = []
    for u in range(NU + 2):
        def qu(u=u):
            if u < NU:
                mainmm(u)
            if u >= 1 and u - 1 < NU:
                ssqmm(u - 1)
            if u >= 2 and u - 2 < NU:
                bcmm(u - 2)
        qunits.append(qu)
    fence = tuple(qkrawB + qksqB + sstdB + rinvB + qkstB + psQKB + psSQB + psBCB)
    return qunits, fence


def _v_section(nc, p, b, T, dr, cb, bufs, hand, qk, fence):
    NKT = T // 128
    cB = bufs["cB"]
    sfx = hand["sfx"]
    xnT, xnTgB = hand["xnT"], hand["xnTgB"]
    qkvb, qkvbB = qk["qkvb"], qk["qkvbB"]
    with ExitStack() as e3v:
        vstg = [e3v.enter_context(nc.sbuf_tensor("vstg%d%s" % (i, sfx), [128, 16 * 520], BF16))
                for i in range(2)]
        vstgB = [[Buf() for _ in range(8)] for _ in range(2)]
        psV = [e3v.enter_context(nc.psum_tensor("psV%d%s" % (i, sfx), [128, D], F32)) for i in range(2)]
        psVB = [Buf() for _ in range(2)]
        _vp(nc, "phAv" + sfx)
        for w in range(2):
            p.op("V", (lambda w_: lambda eng: eng.memset(
                ap2(vstg[w_], 64, [[16 * 520, 128], [65, 128]]), 1.0))(w),
                writes=tuple(vstgB[w]) + fence)
        for qt in range(NKT):
            i2 = qt % 2
            w = (qt // 8) % 2
            sl = qt % 8
            blds = []
            for n in (0, 512):
                for c in range(CC):
                    blds.append((lambda c_, n_, q_, i_: lambda eng: eng.matmul(
                        psV[i_][:, n_:n_ + 512], xnT[c_][:, q_ * 128:(q_ + 1) * 128],
                        qkvb[:, c_ * 3072 + 2048 + n_:c_ * 3072 + 2048 + n_ + 512],
                        start=(c_ == 0), stop=False))(c, n, qt, i2))
                blds.append((lambda n_, i_: lambda eng: eng.matmul(
                    psV[i_][:, n_:n_ + 512], cb["onesb"][:, :],
                    cb["bvrow"][0:1, n_:n_ + 512],
                    start=False, stop=True))(n, i2))
            p.opg("T", blds, reads=(xnTgB[qt // 4], qkvbB, cB),
                  writes=(psVB[i2],) + (fence if qt < 2 else ()))
            p.op("S", (lambda i_, w_, s_: lambda eng: eng.copy(
                ap2(vstg[w_], s_ * 65, [[16 * 520, 128], [520, 16], [1, 64]]),
                psV[i_][:, :]))(i2, w, sl),
                reads=(psVB[i2],), writes=(vstgB[w][sl],))
            if sl == 7 or qt == NKT - 1:
                nsl = sl + 1
                kt0 = qt - (nsl - 1)
                for h in range(NH):
                    p.dma("Y" if h % 2 == 0 else "S",
                          (lambda h_, k0, ns, w_: lambda eng: eng.dma_start(
                              out=ap2(dr["v_d"],
                                      ((b * NH + h_) * 128) * (NKT * 65) + k0 * 65,
                                      [[NKT * 65, 128], [1, ns * 65]]),
                              in_=ap2(vstg[w_], h_ * 520,
                                      [[16 * 520, 128], [1, ns * 65]])))(h, kt0, nsl, w),
                          reads=tuple(vstgB[w][0:nsl]))


def _merged(nc, p, b, T, dr, cb, bufs, hand, w1tb, w1B, dbg, dbgs):
    NKT = T // 128
    SW = min(1024, T)
    NSW = T // SW
    G = 256
    NG = T // G
    cB, mB = bufs["cB"], bufs["mB"]
    sfx = hand["sfx"]
    v1, v1B = hand["v1"], hand["v1B"]
    v2q, v2qB = hand["v2q"], hand["v2qB"]
    oddb, oddbB = hand["oddb"], hand["oddbB"]
    exM = ExitStack()
    w2tb = exM.enter_context(nc.sbuf_tensor("w2tb" + sfx, [128, 32768], BF16, side="right"))
    w2B = Buf()
    if b == 0:
        for c in range(0, CC, 2):
            p.dma("G", (lambda c_: lambda eng: eng.dma_start(
                out=dr["w1c"][:, c_ * HID:(c_ + 2) * HID],
                in_=w1tb[:, c_ * HID:(c_ + 2) * HID]))(c), reads=(w1B,))
    with ExitStack() as em:
        eme = em.enter_context
        xng = [eme(nc.sbuf_tensor("xng%d%s" % (i, sfx), [128, CC * G], BF16)) for i in range(2)]
        xngB = [Buf() for _ in range(2)]
        hT = eme(nc.sbuf_tensor("hT" + sfx, [128, 32 * G], BF16))
        hTgB = [Buf() for _ in range(4)]
        s1s = eme(nc.sbuf_tensor("s1s" + sfx, [4, G], BF16))
        s1sB = Buf()
        ttsq = eme(nc.sbuf_tensor("ttsq" + sfx, [128, G], BF16))
        ttsqB = Buf()
        p.op("V", lambda eng: eng.memset(ttsq[:, :], 0.0), writes=(ttsqB,))
        fstb = eme(nc.sbuf_tensor("fstb" + sfx, [128, D], F32))
        fstbB = Buf()
        qth = [eme(nc.sbuf_tensor("qth%d%s" % (i, sfx), [64, T], BF16)) for i in range(2)]
        qthB = [Buf() for _ in range(2)]
        kth = [eme(nc.sbuf_tensor("kth%d%s" % (i, sfx), [64, T], BF16)) for i in range(2)]
        kthB = [Buf() for _ in range(2)]
        vh = [eme(nc.sbuf_tensor("vh%d%s" % (i, sfx), [128, NKT * 65], BF16)) for i in range(2)]
        vhB = [Buf() for _ in range(2)]
        PT = [eme(nc.sbuf_tensor("PT%d%s" % (i, sfx), [128, SW], BF16)) for i in range(3)]
        PTB = [Buf() for _ in range(3)]
        aost = eme(nc.sbuf_tensor("aost" + sfx, [64, SW], BF16))
        aostB = Buf()
        lst = eme(nc.sbuf_tensor("lst" + sfx, [1, SW], F32))
        lstB = Buf()
        psA = eme(nc.psum_tensor("psA" + sfx, [128, G], F32))
        psAB = Buf()
        psF2 = eme(nc.psum_tensor("psF2" + sfx, [128, 512], F32))
        psF2B = Buf()
        psS = [eme(nc.psum_tensor("psSc%d%s" % (i, sfx), [128, SW], F32)) for i in range(2)]
        psSB = [Buf() for _ in range(2)]
        psPV = eme(nc.psum_tensor("psPV" + sfx, [65, SW], F32))
        psPVB = Buf()
        _vp(nc, "merged" + sfx)

        ffn_units = []

        def ffn_group(g):
            gs = g * G
            us = []
            if g + 1 < NG:
                us.append((lambda g_: lambda: p.dma("G", (lambda gg: lambda eng: eng.dma_start(
                    out=xng[gg % 2][:, :],
                    in_=ap2(dr["xn_d"], ((b * NG + gg) * 128) * (CC * G),
                            [[CC * G, 128], [1, CC * G]])))(g_),
                    writes=(xngB[(g_) % 2],)))(g + 1))
            us.append((lambda g_: lambda: (
                p.opg("T", [(lambda c_, i_: lambda eng: eng.matmul(
                    psA[0:4, :], oddb[:, c_ * 8:c_ * 8 + 4],
                    xng[i_][:, c_ * G:(c_ + 1) * G],
                    start=(c_ == 0), stop=(c_ == CC - 1)))(c, g_ % 2)
                    for c in range(CC)],
                    reads=(xngB[g_ % 2], oddbB), writes=(psAB,)),
                p.op("V", lambda eng: eng.tensor_scalar(
                    s1s[:, :], psA[0:4, :], cb["scolb"][0:4, 0:1], None, op0=ALU.mult),
                    reads=(psAB, cB), writes=(s1sB,))))(g))
            for fb in range(32):
                def u_f1(f_, g_):
                    def run():
                        blds = [(lambda c_, ff, i_: lambda eng: eng.matmul(
                            psA[:, :], w1tb[:, c_ * HID + ff * 128:c_ * HID + (ff + 1) * 128],
                            xng[i_][:, c_ * G:(c_ + 1) * G],
                            start=(c_ == 0), stop=False))(c, f_, g_ % 2)
                            for c in range(CC)]
                        blds.append((lambda ff: lambda eng: eng.matmul(
                            psA[:, :], v1[:, ff * 128:(ff + 1) * 128], s1s[:, :],
                            start=False, stop=True))(f_))
                        p.opg("T", blds, reads=(w1B, xngB[g_ % 2], v1B, s1sB), writes=(psAB,))
                        p.op("V", (lambda ff: lambda eng: eng.tensor_scalar(
                            hT[:, ff * G:(ff + 1) * G], psA[:, :],
                            cb["b1cb"][:, ff:ff + 1], None, op0=ALU.add))(f_),
                            reads=(psAB, cB), writes=(hTgB[f_ // 8],))
                    return run
                us.append(u_f1(fb, g))
            tail = []

            def u_gelu():
                for bb in range(4):
                    p.op("S", (lambda b_: lambda eng: eng.activation(
                        hT[:, b_ * 8 * G:(b_ + 1) * 8 * G],
                        hT[:, b_ * 8 * G:(b_ + 1) * 8 * G], AF.Gelu))(bb),
                        reads=(hTgB[bb],), writes=(hTgB[bb],))
            tail.append(u_gelu)
            for bp in range(4):
                def u_t(b_, g_):
                    def run():
                        p.opg("T", [(lambda bb, j_: lambda eng: eng.matmul(
                            psA[0:4, :], oddb[:, j_ * 8 + 4:j_ * 8 + 8],
                            hT[:, (bb * 8 + j_) * G:(bb * 8 + j_ + 1) * G],
                            start=(j_ == 0), stop=(j_ == CC - 1)))(b_, jc)
                            for jc in range(CC)],
                            reads=(oddbB, hTgB[b_]), writes=(psAB,))
                        p.op("V", (lambda bb: lambda eng: eng.tensor_scalar(
                            ttsq[32 * bb:32 * bb + 4, :], psA[0:4, :], cb["scolb"][0:4, 0:1],
                            None, op0=ALU.mult))(b_),
                            reads=(psAB, cB), writes=(ttsqB,))
                    return run
                tail.append(u_t(bp, g))
            for qtl in range(G // 128):
                for n in (0, 512):
                    def u_f2(q_, n_, g_):
                        def run():
                            qoff = q_ * 128
                            blds = []
                            for hc in range(32):
                                blds.append((lambda h_: lambda eng: eng.matmul(
                                    psF2[:, :],
                                    hT[:, h_ * G + qoff:h_ * G + qoff + 128],
                                    w2tb[:, h_ * D + n_:h_ * D + n_ + 512],
                                    start=(h_ == 0), stop=False))(hc))
                            blds.append(lambda eng: eng.matmul(
                                psF2[:, :], ttsq[:, qoff:qoff + 128],
                                v2q[:, n_:n_ + 512], start=False, stop=False))
                            blds.append(lambda eng: eng.matmul(
                                psF2[:, :], cb["onesb"][:, :],
                                cb["b2row"][0:1, n_:n_ + 512],
                                start=False, stop=True))
                            p.opg("T", blds,
                                  reads=tuple(hTgB) + (w2B, ttsqB, cB, v2qB),
                                  writes=(psF2B,))
                            p.op("V", lambda eng: eng.tensor_copy(
                                fstb[:, n_:n_ + 512], psF2[:, :]),
                                reads=(psF2B,), writes=(fstbB,))
                            if n_ == 512:
                                p.dma("S", (lambda t0: lambda eng: eng.dma_start(
                                    out=dr["ffnd"][b, t0:t0 + 128, :], in_=fstb[:, :]))(g_ * G + qoff),
                                    reads=(fstbB,))
                        return run
                    tail.append(u_f2(qtl, n, g))
            return us, tail

        pending_tail = []
        for g in range(NG):
            us, tail = ffn_group(g)
            ffn_units.extend(pending_tail)
            ffn_units.extend(us)
            pending_tail = tail
        ffn_units.extend(pending_tail)
        total_f = len(ffn_units)
        total_kt = NH * NSW * NKT
        state = {"fe": 0, "ke": 0}

        def pump():
            while (state["fe"] < total_f and
                   state["fe"] * total_kt <= state["ke"] * total_f):
                ffn_units[state["fe"]]()
                state["fe"] += 1

        def stream_head(h_):
            sl = h_ % 2
            p.dma("Y", (lambda hh, s_: lambda eng: eng.dma_start(
                out=kth[s_][:, :], in_=dr["k_d"][b, hh, :, :]))(h_, sl),
                writes=(kthB[sl],))
            p.dma("Y", (lambda hh, s_: lambda eng: eng.dma_start(
                out=qth[s_][:, :], in_=dr["q_d"][b, hh, :, :]))(h_, sl),
                writes=(qthB[sl],))
            p.dma("S", (lambda hh, s_: lambda eng: eng.dma_start(
                out=vh[s_][:, :],
                in_=ap2(dr["v_d"], ((b * NH + hh) * 128) * (NKT * 65),
                        [[NKT * 65, 128], [1, NKT * 65]])))(h_, sl),
                writes=(vhB[sl],))

        p.dma("G", lambda eng: eng.dma_start(
            out=xng[0][:, :],
            in_=ap2(dr["xn_d"], (b * NG) * 128 * (CC * G),
                    [[CC * G, 128], [1, CC * G]])), writes=(xngB[0],))
        stream_head(0)
        stream_head(1)
        if b == 0:
            for hq in range(8):
                p.dma("G", (lambda h_: lambda eng: eng.dma_start(
                    out=w2tb[:, h_ * 4 * D:(h_ + 1) * 4 * D],
                    in_=ap2(dr["w2t"], h_ * 512 * D,
                            [[D, 128], [128 * D, 4], [1, D]])))(hq), writes=(w2B,))
        else:
            for hq in range(0, 8, 2):
                p.dma("G", (lambda h_: lambda eng: eng.dma_start(
                    out=w2tb[:, h_ * 4096:(h_ + 2) * 4096],
                    in_=dr["w2c"][:, h_ * 4096:(h_ + 2) * 4096]))(hq), writes=(w2B,))
        if b == 0:
            for hq in range(0, 8, 2):
                p.dma("G", (lambda h_: lambda eng: eng.dma_start(
                    out=dr["w2c"][:, h_ * 4096:(h_ + 2) * 4096],
                    in_=w2tb[:, h_ * 4096:(h_ + 2) * 4096]))(hq), reads=(w2B,))

        for hh in range(NH):
            sl = hh % 2
            for sw in range(NSW):
                q0 = sw * SW

                def pv(kt):
                    i3 = kt % 3
                    edge = kt == 0 or kt == NKT - 1
                    p.opg("T", [(lambda k_, n_, i_, s_: lambda eng: eng.matmul(
                        psPV[:, n_:n_ + 512],
                        vh[s_][:, k_ * 65:(k_ + 1) * 65],
                        PT[i_][:, n_:n_ + 512],
                        start=(k_ == 0), stop=(k_ == NKT - 1)))(kt, n, i3, sl)
                        for n in range(0, SW, 512)],
                        reads=(PTB[i3], vhB[sl]),
                        writes=(psPVB,) if edge else ())

                for kt in range(NKT):
                    i2 = kt % 2
                    p.opg("T", [(lambda k_, n_, i_, s_, q_: lambda eng: eng.matmul(
                        psS[i_][:, n_:n_ + 512],
                        kth[s_][:, k_ * 128:(k_ + 1) * 128],
                        qth[s_][:, q_ + n_:q_ + n_ + 512],
                        start=True, stop=True))(kt, n, i2, sl, q0)
                        for n in range(0, SW, 512)],
                        reads=(kthB[sl], qthB[sl]), writes=(psSB[i2],))
                    p.op("S", (lambda i_, i3_: lambda eng: eng.activation(
                        PT[i3_][:, :], psS[i_][:, :], AF.Exp,
                        bias=cb["mbias"][:, 0:1], scale=0.125))(i2, kt % 3),
                        reads=(psSB[i2], mB), writes=(PTB[kt % 3],))
                    if kt > 1:
                        pv(kt - 2)
                    state["ke"] += 1
                    pump()
                pv(NKT - 2)
                pv(NKT - 1)
                p.op("V", lambda eng: eng.tensor_copy(aost[:, :], psPV[0:64, :]),
                     reads=(psPVB,), writes=(aostB,))
                p.op("V", lambda eng: eng.tensor_copy(lst[:, :], psPV[64:65, :]),
                     reads=(psPVB,), writes=(lstB,))
                p.dma("Y", (lambda hh_, q_: lambda eng: eng.dma_start(
                    out=dr["ao_d"][b, hh_ >> 1, 64 * (hh_ & 1):64 * (hh_ & 1) + 64, q_:q_ + SW],
                    in_=aost[:, :]))(hh, q0),
                    reads=(aostB,))
                p.dma("S", (lambda hh_, q_: lambda eng: eng.dma_start(
                    out=dr["l_d"][b, hh_, q_:q_ + SW], in_=lst[:, :]))(hh, q0),
                    reads=(lstB,))
            if hh + 2 < NH:
                stream_head(hh + 2)
        while state["fe"] < total_f:
            ffn_units[state["fe"]]()
            state["fe"] += 1
        if dbg:
            p.dma("Y", lambda eng: eng.dma_start(out=dbgs["hT"][:, :], in_=hT[:, :]),
                  reads=tuple(hTgB))
    exM.close()


def _proj_units(nc, p, b, T, dr, cb, bufs, ep, overlap=False):
    NTI = T // 128
    SW = min(1024, T)
    NP = NH // 2
    cB = bufs["cB"]
    sfx = "_p%d" % b
    epe = ep.enter_context
    wprojb = epe(nc.sbuf_tensor("wprojb" + sfx, [128, 8192], BF16))
    wpB = Buf()
    aoTn = epe(nc.sbuf_tensor("aoTn" + sfx, [128, 8 * T], BF16))
    aoTnB = Buf()
    aorw = [epe(nc.sbuf_tensor("aorw%d%s" % (i, sfx), [128, T], BF16)) for i in range(2)]
    aorwB = [Buf() for _ in range(2)]
    lpr = epe(nc.sbuf_tensor("lpr" + sfx, [NH, T], F32))
    lprB = Buf()
    linv16 = epe(nc.sbuf_tensor("linv16" + sfx, [NH, T], BF16))
    linv16B = Buf()
    leo = [epe(nc.sbuf_tensor("leo%d%s" % (i, sfx), [1, 2 * T], BF16)) for i in range(2)]
    leoB = [Buf() for _ in range(2)]
    xres = [epe(nc.sbuf_tensor("xres%d%s" % (i, sfx), [128, D], F32)) for i in range(3)]
    xresB = [Buf() for _ in range(3)]
    fres = [epe(nc.sbuf_tensor("fres%d%s" % (i, sfx), [128, D], F32)) for i in range(3)]
    fresB = [Buf() for _ in range(3)]
    yt = [epe(nc.sbuf_tensor("yt%d%s" % (i, sfx), [128, D], F32)) for i in range(2)]
    ytB = [Buf() for _ in range(2)]
    psBC2 = [epe(nc.psum_tensor("psBC2%d%s" % (i, sfx), [128, SW], F32)) for i in range(2)]
    psBC2B = [Buf() for _ in range(2)]
    psPJ = [epe(nc.psum_tensor("psPJ%d%s" % (i, sfx), [128, 512 if overlap else D], F32))
            for i in range(2)]
    psPJB = [Buf() for _ in range(2)]
    _vp(nc, "proj" + sfx)

    def fetch_res(q_):
        i3_ = q_ % 3
        xq = "S" if q_ % 2 == 1 else "Y"
        fq = "S" if (overlap or q_ % 2 == 0) else "G"
        p.dma(xq, (lambda qq, ii: lambda eng: eng.dma_start(
            out=xres[ii][:, :], in_=dr["xd"][b, qq * 128:(qq + 1) * 128, :]))(q_, i3_),
            writes=(xresB[i3_],))
        p.dma(fq, (lambda qq, ii: lambda eng: eng.dma_start(
            out=fres[ii][:, :], in_=dr["ffnd"][b, qq * 128:(qq + 1) * 128, :]))(q_, i3_),
            writes=(fresB[i3_],))

    units = []

    def wpload():
        for c in range(CC):
            p.dma("G", (lambda c_: lambda eng: eng.dma_start(
                out=wprojb[:, c_ * D:(c_ + 1) * D],
                in_=dr["wproj"][c_ * 128:(c_ + 1) * 128, :]))(c), writes=(wpB,))

    def u0():
        if not overlap:
            wpload()
        for q_ in range(min(3, NTI)):
            fetch_res(q_)
        p.dma("S", lambda eng: eng.dma_start(out=lpr[:, :], in_=dr["l_d"][b, :, :]),
              writes=(lprB,))
        p.op("S", lambda eng: eng.activation(lpr[:, :], lpr[:, :], AF.Ln),
             reads=(lprB,), writes=(lprB,))
        p.op("S", lambda eng: eng.activation(linv16[:, :], lpr[:, :], AF.Exp, scale=-1.0),
             reads=(lprB,), writes=(linv16B,))
    units.append(u0)

    def u_pg(pg):
        i2 = pg % 2
        p.dma("Y", (lambda p_, i_: lambda eng: eng.dma_start(
            out=aorw[i_][:, :],
            in_=ap2(dr["ao_d"], ((b * NP + p_) * 128) * T,
                    [[T, 128], [1, T]])))(pg, i2),
            writes=(aorwB[i2],))
        for r in range(2):
            p.dma("S", (lambda p_, i_, r_: lambda eng: eng.dma_start(
                out=leo[i_][:, r_ * T:(r_ + 1) * T],
                in_=linv16[2 * p_ + r_:2 * p_ + r_ + 1, :]))(pg, i2, r),
                reads=(linv16B,), writes=(leoB[i2],))
        for q0 in range(0, T, SW):
            p.opg("T", [(lambda hf, n_, q_, i_, j_: lambda eng: eng.matmul(
                psBC2[j_][hf:hf + 64, n_:n_ + 512], cb["onesb"][0:1, 0:64],
                leo[i_][0:1, (hf // 64) * T + q_ + n_:(hf // 64) * T + q_ + n_ + 512],
                start=True, stop=True))(half, n, q0, i2, pg % 2)
                for half in (0, 64) for n in range(0, SW, 512)],
                reads=(leoB[i2], cB), writes=(psBC2B[pg % 2],))
            p.op("V", (lambda p_, i_, q_, j_: lambda eng: eng.tensor_tensor(
                aoTn[:, p_ * T + q_:p_ * T + q_ + SW],
                aorw[i_][:, q_:q_ + SW],
                psBC2[j_][:, :], op=ALU.mult))(pg, i2, q0, pg % 2),
                reads=(psBC2B[pg % 2], aorwB[i2]), writes=(aoTnB,))
    for pg in range(NP):
        units.append((lambda p_: lambda: u_pg(p_))(pg))
        if overlap and pg == 2:
            units.append(wpload)

    def u_qt(qt):
        i2 = qt % 2
        i3r = qt % 3
        p.op("V", (lambda i_, j_: lambda eng: eng.tensor_add(
            yt[i_][:, :], xres[j_][:, :], fres[j_][:, :]))(i2, i3r),
            reads=(xresB[i3r], fresB[i3r]), writes=(ytB[i2],))
        if overlap:
            for ih, n in enumerate((0, 512)):
                blds = []
                for c in range(CC):
                    blds.append((lambda c_, n_, q_, i_: lambda eng: eng.matmul(
                        psPJ[i_][:, :], aoTn[:, c_ * T + q_ * 128:c_ * T + (q_ + 1) * 128],
                        wprojb[:, c_ * D + n_:c_ * D + n_ + 512],
                        start=(c_ == 0), stop=False))(c, n, qt, ih))
                blds.append((lambda n_, i_: lambda eng: eng.matmul(
                    psPJ[i_][:, :], cb["onesb"][:, :], cb["bprow"][0:1, n_:n_ + 512],
                    start=False, stop=True))(n, ih))
                p.opg("T", blds, reads=(wpB, cB, aoTnB), writes=(psPJB[ih],))
                p.op("V", (lambda i_, n_, j_: lambda eng: eng.tensor_add(
                    yt[i_][:, n_:n_ + 512], yt[i_][:, n_:n_ + 512], psPJ[j_][:, :]))(i2, n, ih),
                    reads=(psPJB[ih], ytB[i2]), writes=(ytB[i2],))
        else:
            blds = []
            for n in (0, 512):
                for c in range(CC):
                    blds.append((lambda c_, n_, q_, i_: lambda eng: eng.matmul(
                        psPJ[i_][:, n_:n_ + 512], aoTn[:, c_ * T + q_ * 128:c_ * T + (q_ + 1) * 128],
                        wprojb[:, c_ * D + n_:c_ * D + n_ + 512],
                        start=(c_ == 0), stop=False))(c, n, qt, i2))
                blds.append((lambda n_, i_: lambda eng: eng.matmul(
                    psPJ[i_][:, n_:n_ + 512], cb["onesb"][:, :], cb["bprow"][0:1, n_:n_ + 512],
                    start=False, stop=True))(n, i2))
            p.opg("T", blds, reads=(wpB, cB, aoTnB), writes=(psPJB[i2],))
            p.op("V", (lambda i_: lambda eng: eng.tensor_add(
                yt[i_][:, :], yt[i_][:, :], psPJ[i_][:, :]))(i2),
                reads=(psPJB[i2], ytB[i2]), writes=(ytB[i2],))
        p.dma("Y" if qt % 2 == 0 else "S", (lambda q_, i_: lambda eng: eng.dma_start(
            out=dr["yd"][b, q_ * 128:(q_ + 1) * 128, :], in_=yt[i_][:, :]))(qt, i2),
            reads=(ytB[i2],))
        if qt + 3 < NTI:
            fetch_res(qt + 3)
    for qt in range(NTI):
        units.append((lambda q_: lambda: u_qt(q_))(qt))
    return units


def _host_consts():
    bf = ml_dtypes.bfloat16
    ident = np.eye(128, dtype=bf)
    onesb = np.ones((1, 128), dtype=bf)
    sel2 = np.zeros((2, 128), dtype=bf)
    sel2[0, 0:64] = 1
    sel2[1, 64:128] = 1
    sel128 = np.zeros((128, 2), dtype=bf)
    sel128[0:64, 0] = 1
    sel128[64:128, 1] = 1
    return ident, onesb, sel2, sel128


def make_in_maps(inputs, NB, T, n_cores):
    x = np.ascontiguousarray(np.asarray(inputs["L0_attn_x"], dtype=np.float32))
    wqkv = np.ascontiguousarray(np.asarray(inputs["L0_attn_W_qkv"], dtype=np.float32))
    bqkv = np.ascontiguousarray(np.asarray(inputs["L0_attn_b_qkv"], dtype=np.float32))
    wproj = np.ascontiguousarray(np.asarray(inputs["L0_attn_W_proj"], dtype=np.float32))
    bproj = np.ascontiguousarray(np.asarray(inputs["L0_attn_b_proj"], dtype=np.float32))
    qn = np.ascontiguousarray(np.asarray(inputs["L0_attn_qn_w"], dtype=np.float32).reshape(DH, 1))
    kn = np.ascontiguousarray(np.asarray(inputs["L0_attn_kn_w"], dtype=np.float32).reshape(DH, 1))
    w1 = np.asarray(inputs["L0_ffn_W1"], dtype=np.float32)
    b1 = np.asarray(inputs["L0_ffn_b1"], dtype=np.float32)
    w2 = np.asarray(inputs["L0_ffn_W2"], dtype=np.float32)
    b2 = np.ascontiguousarray(np.asarray(inputs["L0_ffn_b2"], dtype=np.float32))
    scale = np.asarray(inputs["L0_ffn_scale"], dtype=np.float32)

    w1t = np.ascontiguousarray(w1.T)
    w2t = np.ascontiguousarray(w2.T)
    b1c = np.ascontiguousarray(b1.reshape(32, 128).T)
    bqkvc = np.ascontiguousarray(bqkv[:2048].reshape(16, 128).T)
    scol = np.ascontiguousarray(np.broadcast_to(scale.reshape(1, 1), (128, 1)))
    ident, onesb, sel2, sel128 = _host_consts()

    xs = x.reshape(n_cores, NB, T, D)
    common = dict(wqkv=wqkv, bqkv=bqkv, bqkvc=bqkvc, wproj=wproj, bproj=bproj,
                  qn=qn, kn=kn, w1t=w1t, b1c=b1c, w2t=w2t, b2=b2, scol=scol,
                  identd=ident, onesbd=onesb, sel2d=sel2,
                  sel128d=sel128)
    return [dict(common, x=np.ascontiguousarray(xs[i])) for i in range(n_cores)]


def kernel(**inputs):
    NB, T, NCORES = 2, 2048, 8
    nc = build_program(NB, T, dbg=False)
    in_maps = make_in_maps(inputs, NB, T, NCORES)
    res = run_bass_kernel_spmd(nc, in_maps, core_ids=list(range(NCORES)))
    y = np.concatenate([np.asarray(r["y"]) for r in res.results], axis=0)
    return np.ascontiguousarray(y.reshape(16, 2048, 1024)).astype(np.float32)


# version 48 (final)
# speedup vs baseline: 1.0800x; 1.0276x over previous
import numpy as np
import ml_dtypes
from contextlib import ExitStack

import concourse.bass as bass
import concourse.mybir as mybir
from concourse.bass_utils import run_bass_kernel_spmd

F32 = mybir.dt.float32
BF16 = mybir.dt.bfloat16
AF = mybir.ActivationFunctionType
ALU = mybir.AluOpType
AX = mybir.AxisListType

D = 1024
NH = 16
DH = 64
HID = 4096
CC = 8
LN_EPS = 1e-5
RMS_EPS = 1e-6


class Buf:
    __slots__ = ("w", "r")

    def __init__(self):
        self.w = None
        self.r = []


class Prog:
    ENGS = ("T", "V", "S", "G", "Y")

    def __init__(self, nc, es):
        self.nc = nc
        self.rec = {e: [] for e in self.ENGS}
        self.sems = {}
        self.cnt = {}
        self.covered = {}
        for e in self.ENGS:
            self._mksem("c" + e, es)
        self.dmaq = {}
        self.dmacnt = {}
        for q, n in (("Y", 6), ("G", 6), ("S", 6)):
            self.dmaq[q] = ["d%s%d" % (q, i) for i in range(n)]
            self.dmacnt[q] = 0
            for s in self.dmaq[q]:
                self._mksem(s, es)

    def _mksem(self, name, es):
        self.sems[name] = es.enter_context(self.nc.semaphore(name))
        self.cnt[name] = 0

    def _waits(self, e, tags):
        out = []
        for (s, v) in tags:
            k = (e, s)
            if self.covered.get(k, 0) >= v:
                continue
            self.covered[k] = v
            out.append((s, v))
        return out

    def _deptags(self, reads, writes):
        tags = []
        for b in reads:
            if b.w:
                tags.append(b.w)
        for b in writes:
            if b.w:
                tags.append(b.w)
            tags.extend(b.r)
        return tags

    def _post(self, tag, reads, writes):
        for b in reads:
            b.r.append(tag)
        for b in writes:
            b.w = tag
            b.r = []

    def op(self, e, build, reads=(), writes=()):
        w = self._waits(e, self._deptags(reads, writes))
        s = "c" + e
        self.cnt[s] += 1
        self.rec[e].append((w, build, s, self.cnt[s], 1))
        tag = (s, self.cnt[s])
        self._post(tag, reads, writes)
        return tag

    def opg(self, e, builds, reads=(), writes=()):
        w = self._waits(e, self._deptags(reads, writes))
        s = "c" + e
        self.cnt[s] += 1
        v = self.cnt[s]
        n = len(builds)
        for i, bld in enumerate(builds):
            self.rec[e].append((w if i == 0 else [],
                                bld,
                                s if i == n - 1 else None, v, 1))
        tag = (s, v)
        self._post(tag, reads, writes)
        return tag

    def dma(self, q, build, reads=(), writes=()):
        i = self.dmacnt[q]
        self.dmacnt[q] += 1
        s = self.dmaq[q][i % len(self.dmaq[q])]
        w = self._waits(q, self._deptags(reads, writes) + [(s, self.cnt[s])])
        self.cnt[s] += 16
        self.rec[q].append((w, build, s, self.cnt[s], 16))
        tag = (s, self.cnt[s])
        self._post(tag, reads, writes)
        return tag

    def barrier(self, skip_qs=()):
        snap = dict(self.cnt)
        skip = tuple("d" + q for q in skip_qs)
        for e in self.ENGS:
            tags = [("c" + o, snap["c" + o]) for o in self.ENGS if o != e]
            tags += [(s, snap[s]) for s in snap
                     if s.startswith("d") and not s.startswith(skip)]
            w = self._waits(e, [t for t in tags if t[1] > 0])
            if w:
                self.rec[e].append((w, None, None, None, None))

    def emit(self, block):
        sems = self.sems

        def body_for(e):
            items = self.rec[e]

            def body(eng):
                for (w, build, s, v, inc) in items:
                    for (sn, vv) in w:
                        eng.wait_ge(sems[sn], vv)
                    if build is not None:
                        ins = build(eng)
                        if s is not None:
                            ins.then_inc(sems[s], inc)
            return body

        block.tensor(body_for("T"))
        block.vector(body_for("V"))
        block.scalar(body_for("S"))
        block.gpsimd(body_for("G"))
        block.sync(body_for("Y"))


def ap2(t, offset, dims):
    return bass.AP(t, offset, dims)


VERBOSE = [False]


def _vp(nc, what):
    if VERBOSE[0]:
        print("[sbuf] %-14s base=%7d  used_KB=%.1f" % (what, nc.sbuf_base, (nc.sbuf_base - 16481) / 1024.0))


def build_program(NB, T, dbg=False, verbose=False):
    VERBOSE[0] = verbose
    """One core's full-forward program.  NB batch elements of T tokens."""
    nc = bass.Bass("TRN2", target_bir_lowering=False)
    NTI = T // 128
    NKT = T // 128
    SW = min(1024, T)
    NSW = T // SW
    G = 256
    NG = T // G
    NQG = T // 512

    xd = nc.dram_tensor("x", [NB, T, D], F32, kind="ExternalInput")
    wqkv = nc.dram_tensor("wqkv", [D, 3 * D], F32, kind="ExternalInput")
    bqkv = nc.dram_tensor("bqkv", [3 * D], F32, kind="ExternalInput")
    bqkvc = nc.dram_tensor("bqkvc", [128, 16], F32, kind="ExternalInput")
    wproj = nc.dram_tensor("wproj", [D, D], F32, kind="ExternalInput")
    bproj = nc.dram_tensor("bproj", [D], F32, kind="ExternalInput")
    qn = nc.dram_tensor("qn", [DH, 1], F32, kind="ExternalInput")
    kn = nc.dram_tensor("kn", [DH, 1], F32, kind="ExternalInput")
    w1t = nc.dram_tensor("w1t", [D, HID], F32, kind="ExternalInput")
    b1c = nc.dram_tensor("b1c", [128, 32], F32, kind="ExternalInput")
    w2t = nc.dram_tensor("w2t", [HID, D], F32, kind="ExternalInput")
    b2 = nc.dram_tensor("b2", [D], F32, kind="ExternalInput")
    scol = nc.dram_tensor("scol", [128, 1], F32, kind="ExternalInput")
    identd = nc.dram_tensor("identd", [128, 128], BF16, kind="ExternalInput")
    onesbd = nc.dram_tensor("onesbd", [1, 128], BF16, kind="ExternalInput")
    sel2d = nc.dram_tensor("sel2d", [2, 128], BF16, kind="ExternalInput")
    sel128d = nc.dram_tensor("sel128d", [128, 2], BF16, kind="ExternalInput")
    yd = nc.dram_tensor("y", [NB, T, D], F32, kind="ExternalOutput")
    dk = "ExternalOutput" if dbg else "Internal"
    ffnd = nc.dram_tensor("ffnscr", [NB, T, D], F32, kind=dk)
    gbounce = nc.dram_tensor("gbounce", [DH, 1], F32, kind="Internal")
    w1c_d = nc.dram_tensor("w1c_d", [128, 32768], BF16, kind="Internal")
    w2c_d = nc.dram_tensor("w2c_d", [128, 32768], BF16, kind="Internal")
    qkvc_d = nc.dram_tensor("qkvc_d", [128, 24576], BF16, kind="Internal")
    xn_d = nc.dram_tensor("xn_d", [NB, NG, 128, CC * G], BF16, kind=dk)
    q_d = nc.dram_tensor("q_d", [NB, NH, 64, T], BF16, kind=dk)
    k_d = nc.dram_tensor("k_d", [NB, NH, 64, T], BF16, kind=dk)
    v_d = nc.dram_tensor("v_d", [NB, NH, 128, NKT * 65], BF16, kind=dk)
    ao_d = nc.dram_tensor("ao_d", [NB, NH // 2, 128, T], BF16, kind=dk)
    l_d = nc.dram_tensor("l_d", [NB, NH, T], F32, kind=dk)
    dbgs = {}
    dbg_v1 = dbg_v2q = None
    if dbg:
        dbgs["hT"] = nc.dram_tensor("dbg_hT", [128, 32 * G], BF16, kind="ExternalOutput")
        dbgs["mb"] = nc.dram_tensor("dbg_mb", [128, 1], F32, kind="ExternalOutput")
        dbg_v1 = nc.dram_tensor("dbg_v1", [4, HID], BF16, kind="ExternalOutput")
        dbg_v2q = nc.dram_tensor("dbg_v2q", [128, D], BF16, kind="ExternalOutput")

    with ExitStack() as es0:
        p = Prog(nc, es0)
        e0 = es0.enter_context

        ident = e0(nc.sbuf_tensor("ident", [128, 128], BF16))
        onesb = e0(nc.sbuf_tensor("onesb", [1, 128], BF16))

        sel2 = e0(nc.sbuf_tensor("sel2", [2, 128], BF16))
        sel128 = e0(nc.sbuf_tensor("sel128", [128, 2], BF16))
        scolb = e0(nc.sbuf_tensor("scolb", [128, 1], F32))
        b1cb = e0(nc.sbuf_tensor("b1cb", [128, 32], F32))
        bqkvcb = e0(nc.sbuf_tensor("bqkvcb", [128, 16], F32))
        bvrow = e0(nc.sbuf_tensor("bvrow", [1, D], BF16))
        b2row = e0(nc.sbuf_tensor("b2row", [1, D], BF16))
        bprow = e0(nc.sbuf_tensor("bprow", [1, D], BF16))
        qns = e0(nc.sbuf_tensor("qns", [DH, 1], F32))
        kns = e0(nc.sbuf_tensor("kns", [DH, 1], F32))
        gcol = e0(nc.sbuf_tensor("gcol", [DH, 1], F32))
        gtil = e0(nc.sbuf_tensor("gtil", [128, 1], F32))
        gabs = e0(nc.sbuf_tensor("gabs", [DH, 1], F32))
        grow = e0(nc.sbuf_tensor("grow", [1, DH], F32))
        gmax = e0(nc.sbuf_tensor("gmax", [1, 1], F32))
        mneg = e0(nc.sbuf_tensor("mneg", [1, 1], BF16))
        mbias = e0(nc.sbuf_tensor("mbias", [128, 1], F32))
        epsb = e0(nc.sbuf_tensor("epsb", [128, 2], F32))

        cB = Buf()
        gB = Buf()
        mB = Buf()

        for (dst, src) in ((ident, identd), (onesb, onesbd),
                           (sel2, sel2d), (sel128, sel128d), (scolb, scol),
                           (b1cb, b1c), (bqkvcb, bqkvc), (qns, qn), (kns, kn)):
            p.dma("S", (lambda d_, s_: lambda eng: eng.dma_start(out=d_[:, :], in_=s_[:, :]))(dst, src),
                  writes=(cB,))
        p.dma("G", lambda eng: eng.dma_start(
            out=bvrow[:, :], in_=ap2(bqkv, 2048, [[D, 1], [1, D]])), writes=(cB,))
        p.dma("G", lambda eng: eng.dma_start(
            out=b2row[:, :], in_=ap2(b2, 0, [[D, 1], [1, D]])), writes=(cB,))
        p.dma("G", lambda eng: eng.dma_start(
            out=bprow[:, :], in_=ap2(bproj, 0, [[D, 1], [1, D]])), writes=(cB,))

        p.op("V", lambda eng: eng.memset(epsb[:, 0:1], LN_EPS), writes=(cB,))
        p.op("V", lambda eng: eng.memset(epsb[:, 1:2], RMS_EPS), writes=(cB,))

        p.op("V", lambda eng: eng.tensor_mul(gcol[:, :], qns[:, :], kns[:, :]),
             reads=(cB,), writes=(gB,))
        p.op("V", lambda eng: eng.tensor_copy(gtil[0:DH, 0:1], gcol[:, :]),
             reads=(gB,), writes=(gB,))
        p.op("V", lambda eng: eng.tensor_copy(gtil[DH:128, 0:1], gcol[:, :]),
             reads=(gB,), writes=(gB,))
        p.op("S", lambda eng: eng.activation(gabs[:, :], gcol[:, :], AF.Abs),
             reads=(gB,), writes=(gB,))
        p.dma("S", lambda eng: eng.dma_start(out=gbounce[:, :], in_=gabs[:, :]),
              reads=(gB,), writes=(gB,))
        p.dma("S", lambda eng: eng.dma_start(
            out=grow[:, :], in_=ap2(gbounce, 0, [[DH, 1], [1, DH]])),
            reads=(gB,), writes=(gB,))
        p.op("V", lambda eng: eng.reduce_max(gmax[:, :], grow[:, :], axis=AX.X),
             reads=(gB,), writes=(gB,))
        p.op("V", lambda eng: eng.tensor_scalar(mneg[:, :], gmax[:, :], -8.0, None, op0=ALU.mult),
             reads=(gB,), writes=(gB,))
        with nc.psum_tensor("psM", [128, 1], F32) as psM:
            psMB = Buf()
            p.op("T", lambda eng: eng.matmul(psM[:, :], onesb[:, :], mneg[:, :],
                                             start=True, stop=True),
                 reads=(cB, gB), writes=(psMB,))
            p.op("V", lambda eng: eng.tensor_copy(mbias[:, :], psM[:, :]),
                 reads=(psMB,), writes=(mB,))
        if dbg:
            p.dma("Y", lambda eng: eng.dma_start(out=dbgs["mb"][:, :], in_=mbias[:, :]),
                  reads=(mB,))


        dr = dict(xd=xd, wqkv=wqkv, wproj=wproj, w1t=w1t, w2t=w2t,
                  yd=yd, ffnd=ffnd, xn_d=xn_d, q_d=q_d, k_d=k_d, v_d=v_d,
                  ao_d=ao_d, l_d=l_d, w1c=w1c_d, w2c=w2c_d, qkvc=qkvc_d,
                  dbg_v1=dbg_v1, dbg_v2q=dbg_v2q)
        cb = dict(ident=ident, onesb=onesb, sel2=sel2,
                  sel128=sel128, scolb=scolb, b1cb=b1cb, bqkvcb=bqkvcb,
                  bvrow=bvrow, b2row=b2row, bprow=bprow, gtil=gtil,
                  mbias=mbias, epsb=epsb)
        bufs = dict(cB=cB, mB=mB, gB=gB)

        eb = ExitStack()
        hand = _alloc_hand(nc, eb, 0, T)
        exW = eb.enter_context(ExitStack())
        w1tb, w1B = _alloc_w1(nc, exW, "_b0")
        ea = eb.enter_context(ExitStack())
        qk = _alloc_qkv(nc, ea, "_b0")
        qls = ExitStack()
        lnu, lnfence, lnxnbB = _ln_units(nc, p, 0, T, dr, cb, bufs, hand, qls, "Y", "Y")
        qlu = _qkv_load_units(nc, p, 0, dr, qk, pace=[lnxnbB[c % 3] for c in range(CC)])
        qvu, qfence = _qkv_units(nc, p, 0, T, dr, cb, bufs, hand, qk, qls)
        qi = 0
        wdone = [False]

        def _qstep():
            qvu[qi]()
            if qi + 1 >= min(24, len(qvu)) and not wdone[0]:
                _emit_w1_load(nc, p, 0, dr, w1tb, w1B)
                _emit_qkv_cache_write(nc, p, dr, qk)
                wdone[0] = True
        INTERLEAVE = False
        for i, u in enumerate(lnu):
            ready = (min(len(qvu), 16 * max(0, (i - 3) // 4))
                     if (INTERLEAVE and i > len(qlu)) else 0)
            while qi < min(ready, qi + 6):
                _qstep()
                qi += 1
            u()
            if 1 <= i <= len(qlu):
                qlu[i - 1]()
        for c in range(min(len(lnu) - 1, len(qlu)), len(qlu)):
            qlu[c]()
        while qi < len(qvu):
            _qstep()
            qi += 1
        qls.close()

        for b in range(NB):
            p.barrier(skip_qs=("G",))
            _v_section(nc, p, b, T, dr, cb, bufs, hand, qk, qfence + lnfence)
            ea.close()
            hand["xstk"].close()
            p.barrier()
            _merged(nc, p, b, T, dr, cb, bufs, hand, w1tb, w1B, dbg, dbgs)
            exW.close()
            eb.close()
            p.barrier()
            if b + 1 < NB:
                eb = ExitStack()
                hand = _alloc_hand(nc, eb, b + 1, T)
                with ExitStack() as ep:
                    pun = _proj_units(nc, p, b, T, dr, cb, bufs, ep, overlap=True)
                    with ExitStack() as ln:
                        lun, lnfence, _xb = _ln_units(nc, p, b + 1, T, dr, cb, bufs,
                                                      hand, ln, "G", "S")
                        li, nl, npu = 1, len(lun), len(pun)
                        for i, u in enumerate(pun):
                            u()
                            if i == 0:
                                lun[0]()
                            want = 1 + ((i + 1) * (nl - 1)) // npu
                            while li < want:
                                lun[li]()
                                li += 1
                        while li < nl:
                            lun[li]()
                            li += 1
                p.barrier()
                exW = eb.enter_context(ExitStack())
                w1tb, w1B = _alloc_w1(nc, exW, "_b%d" % (b + 1))
                ea = eb.enter_context(ExitStack())
                qk = _alloc_qkv(nc, ea, "_b%d" % (b + 1))
                for qlun in _qkv_load_units(nc, p, b + 1, dr, qk):
                    qlun()
                _emit_w1_load(nc, p, b + 1, dr, w1tb, w1B)
                qls = ExitStack()
                qvu, qfence = _qkv_units(nc, p, b + 1, T, dr, cb, bufs, hand, qk, qls)
                for u in qvu:
                    u()
                qls.close()
            else:
                with ExitStack() as ep:
                    for u in _proj_units(nc, p, b, T, dr, cb, bufs, ep,
                                         overlap=False):
                        u()
                p.barrier()

        with nc.allow_low_precision(reason="bf16 matmul operands by design; fp32 accumulation"):
            with nc.Block() as block:
                p.emit(block)
    return nc


def _alloc_hand(nc, stk, b, T):
    NQG = T // 512
    sfx = "_b%d" % b
    hand = dict(sfx=sfx)
    hand["v1"] = stk.enter_context(nc.sbuf_tensor("v1" + sfx, [4, HID], BF16))
    hand["v1B"] = Buf()
    hand["v2q"] = stk.enter_context(nc.sbuf_tensor("v2q" + sfx, [128, D], BF16))
    hand["v2qB"] = Buf()
    hand["oddb"] = stk.enter_context(nc.sbuf_tensor("oddb" + sfx, [128, 8 * CC], BF16))
    hand["oddbB"] = Buf()
    hand["xstk"] = stk.enter_context(ExitStack())
    hand["xnT"] = [hand["xstk"].enter_context(
        nc.sbuf_tensor("xnT%d%s" % (c, sfx), [128, T], BF16)) for c in range(CC)]
    hand["xnTgB"] = [Buf() for _ in range(NQG)]
    return hand


def _alloc_w1(nc, stk, sfx):
    w1tb = stk.enter_context(nc.sbuf_tensor("w1tb" + sfx, [128, 32768], BF16, side="right"))
    return w1tb, Buf()


def _emit_w1_load(nc, p, b, dr, w1tb, w1B):
    if b == 0:
        for c in range(CC):
            p.dma("G", (lambda c_: lambda eng: eng.dma_start(
                out=w1tb[:, c_ * HID:(c_ + 1) * HID],
                in_=dr["w1t"][c_ * 128:(c_ + 1) * 128, :]))(c), writes=(w1B,))
    else:
        for c in range(0, CC, 2):
            p.dma("G", (lambda c_: lambda eng: eng.dma_start(
                out=w1tb[:, c_ * HID:(c_ + 2) * HID],
                in_=dr["w1c"][:, c_ * HID:(c_ + 2) * HID]))(c), writes=(w1B,))


def _alloc_qkv(nc, stk, sfx):
    qk = {}
    qk["qkvb"] = stk.enter_context(nc.sbuf_tensor("qkvb" + sfx, [128, 24576], BF16))
    qk["qkvbB"] = Buf()
    return qk


def _qkv_load_units(nc, p, b, dr, qk, pace=None):
    qkvb, qkvbB = qk["qkvb"], qk["qkvbB"]
    units = []
    if b == 0:
        for c in range(CC):
            units.append((lambda c_: lambda: p.dma("G", (lambda cc: lambda eng: eng.dma_start(
                out=qkvb[:, cc * 3072:(cc + 1) * 3072],
                in_=dr["wqkv"][cc * 128:(cc + 1) * 128, :]))(c_),
                reads=(pace[c_],) if pace else (),
                writes=(qkvbB,)))(c))
    else:
        for c in range(0, CC, 2):
            units.append((lambda c_: lambda: p.dma("G", (lambda cc: lambda eng: eng.dma_start(
                out=qkvb[:, cc * 3072:(cc + 2) * 3072],
                in_=dr["qkvc"][:, cc * 3072:(cc + 2) * 3072]))(c_),
                writes=(qkvbB,)))(c))
    return units


def _emit_qkv_cache_write(nc, p, dr, qk):
    qkvb, qkvbB = qk["qkvb"], qk["qkvbB"]
    for c in range(0, CC, 2):
        p.dma("G", (lambda c_: lambda eng: eng.dma_start(
            out=dr["qkvc"][:, c_ * 3072:(c_ + 2) * 3072],
            in_=qkvb[:, c_ * 3072:(c_ + 2) * 3072]))(c), reads=(qkvbB,))


def _ln_units(nc, p, b, T, dr, cb, bufs, hand, lnstk, xq, wq):
    NTI = T // 128
    NG = T // 256
    G = 256
    cB = bufs["cB"]
    sfx = hand["sfx"]
    xnT, xnTgB = hand["xnT"], hand["xnTgB"]
    v1, v1B = hand["v1"], hand["v1B"]
    v2q, v2qB = hand["v2q"], hand["v2qB"]
    oddb, oddbB = hand["oddb"], hand["oddbB"]
    eae = lnstk.enter_context
    xt = [eae(nc.sbuf_tensor("xt%d%s" % (i, sfx), [128, D], F32)) for i in range(3)]
    xtB = [Buf() for _ in range(3)]
    xnb = [eae(nc.sbuf_tensor("xnb%d%s" % (i, sfx), [128, D], BF16)) for i in range(3)]
    xnbB = [Buf() for _ in range(3)]
    xsq = [eae(nc.sbuf_tensor("xsq%d%s" % (i, sfx), [128, D], BF16)) for i in range(1)]
    xsqB = [Buf() for _ in range(1)]
    st1 = [eae(nc.sbuf_tensor("st1%d%s" % (i, sfx), [128, 8], F32)) for i in range(3)]
    st1B = [Buf() for _ in range(3)]
    l04 = eae(nc.sbuf_tensor("l04" + sfx, [4, D], BF16))
    l04B = Buf()
    lq4 = eae(nc.sbuf_tensor("lq4" + sfx, [4, D], BF16))
    lq4B = Buf()
    psTP = [eae(nc.psum_tensor("psTP%d%s" % (i, sfx), [128, 1024], BF16)) for i in range(2)]
    psTPB = [Buf() for _ in range(2)]
    _vp(nc, "phA1" + sfx)

    def _sl(t):
        ist = t % 3
        return dict(
            s1a=st1[ist][:, 0:1], s2a=st1[ist][:, 1:2],
            mea=st1[ist][:, 2:3], m2a=st1[ist][:, 3:4],
            vaa=st1[ist][:, 4:5], sda=st1[ist][:, 5:6],
            rsa=st1[ist][:, 6:7], nma=st1[ist][:, 7:8],
            sB=st1B[ist], xB=xtB[t % 3], nB=xnbB[t % 3], qB=xsqB[0])

    def stA(t):
        L = _sl(t)
        p.dma(xq, (lambda t_, i_: lambda eng: eng.dma_start(
            out=xt[i_][:, :], in_=dr["xd"][b, t_ * 128:(t_ + 1) * 128, :]))(t, t % 3),
            writes=(L["xB"],))
        p.op("S", (lambda i_, a: lambda eng: eng.activation(
            xsq[0][:, :], xt[i_][:, :], AF.Identity, accum_out=a))(t % 3, L["s1a"]),
            reads=(L["xB"],), writes=(L["qB"], L["sB"]))
        p.op("S", (lambda i_, a: lambda eng: eng.activation(
            xsq[0][:, :], xt[i_][:, :], AF.Square, accum_out=a))(t % 3, L["s2a"]),
            reads=(L["xB"], L["qB"]), writes=(L["qB"], L["sB"]))

    def stB(t):
        L = _sl(t)
        p.op("V", (lambda a, c: lambda eng: eng.tensor_scalar(
            a, c, 1.0 / D, None, op0=ALU.mult))(L["mea"], L["s1a"]),
            reads=(L["sB"],), writes=(L["sB"],))
        p.op("V", (lambda a, c: lambda eng: eng.tensor_mul(a, c, c))(L["m2a"], L["mea"]),
             reads=(L["sB"],), writes=(L["sB"],))
        p.op("V", (lambda a, c, d2: lambda eng: eng.tensor_scalar(
            a, c, 1.0 / D, d2, op0=ALU.mult, op1=ALU.subtract))(L["vaa"], L["s2a"], L["m2a"]),
            reads=(L["sB"],), writes=(L["sB"],))

    def stC(t):
        L = _sl(t)
        p.op("S", (lambda a, c: lambda eng: eng.activation(
            a, c, AF.Ln, bias=cb["epsb"][:, 0:1], scale=1.0))(L["sda"], L["vaa"]),
            reads=(L["sB"], cB), writes=(L["sB"],))
        p.op("S", (lambda a, c: lambda eng: eng.activation(
            a, c, AF.Exp, scale=-0.5))(L["rsa"], L["sda"]),
            reads=(L["sB"],), writes=(L["sB"],))

    def stD(t):
        L = _sl(t)
        p.op("V", (lambda a, c, d2: lambda eng: eng.tensor_scalar(
            a, c, d2, -1.0, op0=ALU.mult, op1=ALU.mult))(L["nma"], L["mea"], L["rsa"]),
            reads=(L["sB"],), writes=(L["sB"],))
        p.op("S", (lambda i_, j_, r_, n_: lambda eng: eng.activation(
            xnb[j_][:, :], xt[i_][:, :], AF.Identity, bias=n_, scale=r_))(
                t % 3, t % 3, L["rsa"], L["nma"]),
            reads=(L["xB"], L["sB"]), writes=(L["nB"],))

    def stE(t):
        L = _sl(t)
        if t == 0:
            p.op("V", lambda eng: eng.memset(v1[:, :], 0.0), writes=(v1B,))
            p.op("V", lambda eng: eng.memset(v2q[:, :], 0.0), writes=(v2qB,))
            for r in range(4):
                p.dma("S", (lambda r_, j_: lambda eng: eng.dma_start(
                    out=v1[r_:r_ + 1, r_ * D:(r_ + 1) * D],
                    in_=xnb[j_][1 + 2 * r_:2 + 2 * r_, :]))(r, t % 3),
                    reads=(L["nB"],), writes=(v1B,))
                p.dma("S", (lambda r_, j_: lambda eng: eng.dma_start(
                    out=l04[r_:r_ + 1, :],
                    in_=xnb[j_][9 + 2 * r_:10 + 2 * r_, :]))(r, t % 3),
                    reads=(L["nB"],), writes=(l04B,))
            for q in range(4):
                p.op("V", (lambda q_: lambda eng: eng.tensor_copy(
                    lq4[0:4, q_ * 256:(q_ + 1) * 256],
                    ap2(l04, q_, [[D, 4], [4, 256]])))(q),
                    reads=(l04B,), writes=(lq4B,))
            for q in range(4):
                for r in range(4):
                    p.dma("S", (lambda q_, r_: lambda eng: eng.dma_start(
                        out=v2q[32 * q_ + r_:32 * q_ + r_ + 1, 256 * r_:256 * (r_ + 1)],
                        in_=lq4[r_:r_ + 1, 256 * q_:256 * (q_ + 1)]))(q, r),
                        reads=(lq4B,), writes=(v2qB,))
            if dr.get("dbg_v1") is not None:
                p.dma("S", lambda eng: eng.dma_start(out=dr["dbg_v1"][:, :], in_=v1[:, :]),
                      reads=(v1B,))
                p.dma("S", lambda eng: eng.dma_start(out=dr["dbg_v2q"][:, :], in_=v2q[:, :]),
                      reads=(v2qB,))
        p.opg("T", [(lambda j_, c_, k_: lambda eng: eng.transpose(
            psTP[k_][:, c_ * 128:(c_ + 1) * 128],
            xnb[j_][:, c_ * 128:(c_ + 1) * 128], cb["ident"][:, :]))(t % 3, c, t % 2)
            for c in range(CC)],
            reads=(L["nB"], cB), writes=(psTPB[t % 2],))

    def stF(t):
        for c in range(CC):
            p.op("V", (lambda c_, t_, k_: lambda eng: eng.tensor_copy(
                xnT[c_][:, t_ * 128:(t_ + 1) * 128],
                psTP[k_][:, c_ * 128:(c_ + 1) * 128]))(c, t, t % 2),
                reads=(psTPB[t % 2],), writes=(xnTgB[t // 4],))

    units = []
    for t in range(NTI + 3):
        def step(t=t):
            if t < NTI:
                stA(t)
                stB(t)
            if 1 <= t <= NTI:
                stC(t - 1)
                stD(t - 1)
            if 2 <= t <= NTI + 1:
                stE(t - 2)
            if 3 <= t <= NTI + 2:
                stF(t - 3)
        units.append(step)

    def fin():
        for c in range(CC):
            p.op("V", (lambda c_: lambda eng: eng.tensor_copy(
                oddb[:, c_ * 8:c_ * 8 + 4], xnT[c_][:, 2:10:2]))(c),
                reads=(xnTgB[0],), writes=(oddbB,))
            p.op("V", (lambda c_: lambda eng: eng.tensor_copy(
                oddb[:, c_ * 8 + 4:c_ * 8 + 8], xnT[c_][:, 10:18:2]))(c),
                reads=(xnTgB[0],), writes=(oddbB,))
        for c in range(CC):
            p.dma(wq, (lambda c_: lambda eng: eng.dma_start(
                out=ap2(dr["xn_d"], (b * NG) * 128 * CC * G + c_ * G,
                        [[CC * G, 128], [128 * CC * G, NG], [1, G]]),
                in_=ap2(xnT[c_], 0, [[T, 128], [G, NG], [1, G]])))(c),
                reads=tuple(xnTgB))
    units.append(fin)
    lnfence = tuple(xtB + xnbB + xsqB + st1B + [l04B, lq4B] + psTPB)
    return units, lnfence, xnbB


def _qkv_units(nc, p, b, T, dr, cb, bufs, hand, qk, qstk):
    NKT = T // 128
    NQG = T // 512
    cB, gB = bufs["cB"], bufs["gB"]
    sfx = hand["sfx"]
    xnT, xnTgB = hand["xnT"], hand["xnTgB"]
    qkvb, qkvbB = qk["qkvb"], qk["qkvbB"]
    e3ie = qstk.enter_context
    qkraw = [e3ie(nc.sbuf_tensor("qkraw%d%s" % (i, sfx), [128, 512], F32)) for i in range(3)]
    qkrawB = [Buf() for _ in range(3)]
    qksq = [e3ie(nc.sbuf_tensor("qksq%d%s" % (i, sfx), [128, 512], BF16)) for i in range(2)]
    qksqB = [Buf() for _ in range(2)]
    sstd = [e3ie(nc.sbuf_tensor("sstd%d%s" % (i, sfx), [2, 512], F32)) for i in range(2)]
    sstdB = [Buf() for _ in range(2)]
    rinv = [e3ie(nc.sbuf_tensor("rinv%d%s" % (i, sfx), [2, 512], BF16)) for i in range(3)]
    rinvB = [Buf() for _ in range(3)]
    qkst = [e3ie(nc.sbuf_tensor("qkst%d%s" % (i, sfx), [128, 512], BF16)) for i in range(3)]
    qkstB = [Buf() for _ in range(3)]
    psQK = [e3ie(nc.psum_tensor("psQK%d%s" % (i, sfx), [128, 512], F32)) for i in range(2)]
    psQKB = [Buf() for _ in range(2)]
    psSQ = [e3ie(nc.psum_tensor("psSQ%d%s" % (i, sfx), [2, 512], F32)) for i in range(2)]
    psSQB = [Buf() for _ in range(2)]
    psBC = [e3ie(nc.psum_tensor("psBC%d%s" % (i, sfx), [128, 512], F32)) for i in range(2)]
    psBCB = [Buf() for _ in range(2)]
    _vp(nc, "phA3" + sfx)

    units = [(g, fb) for g in range(NQG) for fb in range(16)]
    NU = len(units)

    def mainmm(u):
        g, fb = units[u]
        i2 = u % 2
        p.opg("T", [(lambda c_, f_, g_, i_: lambda eng: eng.matmul(
            psQK[i_][:, :], qkvb[:, c_ * 3072 + f_ * 128:c_ * 3072 + (f_ + 1) * 128],
            xnT[c_][:, g_ * 512:(g_ + 1) * 512],
            start=(c_ == 0), stop=(c_ == CC - 1)))(c, fb, g, i2)
            for c in range(CC)],
            reads=(qkvbB, xnTgB[g]), writes=(psQKB[i2],))
        p.op("V", (lambda f_, i_, i3: lambda eng: eng.tensor_scalar(
            qkraw[i3][:, :], psQK[i_][:, :],
            cb["bqkvcb"][:, f_:f_ + 1], None, op0=ALU.add))(fb, i2, u % 3),
            reads=(psQKB[i2], cB), writes=(qkrawB[u % 3],))
        p.op("V", (lambda i3, j_: lambda eng: eng.tensor_mul(
            qksq[j_][:, :], qkraw[i3][:, :], qkraw[i3][:, :]))(u % 3, u % 2),
            reads=(qkrawB[u % 3],), writes=(qksqB[u % 2],))

    def ssqmm(u):
        j = u % 2
        p.op("T", (lambda j_: lambda eng: eng.matmul(
            psSQ[j_][:, :], cb["sel128"][:, :], qksq[j_][:, :],
            start=True, stop=True))(j),
            reads=(qksqB[j], cB), writes=(psSQB[j],))
        p.op("S", (lambda j_: lambda eng: eng.activation(
            sstd[j_][:, :], psSQ[j_][:, :], AF.Ln,
            bias=cb["epsb"][0:2, 1:2], scale=1.0 / DH))(j),
            reads=(psSQB[j], cB), writes=(sstdB[j],))
        p.op("S", (lambda j_, k_: lambda eng: eng.activation(
            rinv[k_][:, :], sstd[j_][:, :], AF.Exp, scale=-0.5))(j, u % 3),
            reads=(sstdB[j],), writes=(rinvB[u % 3],))

    def bcmm(u):
        g, fb = units[u]
        i3, k3, j2, s3 = u % 3, u % 3, u % 2, u % 3
        p.op("T", (lambda k_, j_: lambda eng: eng.matmul(
            psBC[j_][:, :], cb["sel2"][:, :], rinv[k_][:, :],
            start=True, stop=True))(k3, j2),
            reads=(rinvB[k3], cB), writes=(psBCB[j2],))
        if fb < 8:
            p.op("V", (lambda i_, j_, s_: lambda eng: eng.tensor_tensor(
                qkst[s_][:, :], qkraw[i_][:, :], psBC[j_][:, :], op=ALU.mult))(i3, j2, s3),
                reads=(qkrawB[i3], psBCB[j2]), writes=(qkstB[s3],))
            dst = dr["q_d"]
        else:
            p.op("V", (lambda i_, j_, s_: lambda eng: eng.scalar_tensor_tensor(
                qkst[s_][:, :], qkraw[i_][:, :], cb["gtil"][:, 0:1], psBC[j_][:, :],
                op0=ALU.mult, op1=ALU.mult))(i3, j2, s3),
                reads=(qkrawB[i3], psBCB[j2], gB), writes=(qkstB[s3],))
            dst = dr["k_d"]
        h0 = 2 * (fb % 8)
        p.dma("Y", (lambda s_, h_, d_, g_: lambda eng: eng.dma_start(
            out=d_[b, h_, :, g_ * 512:(g_ + 1) * 512], in_=qkst[s_][0:64, :]))(s3, h0, dst, g),
            reads=(qkstB[s3],))
        p.dma("S", (lambda s_, h_, d_, g_: lambda eng: eng.dma_start(
            out=d_[b, h_ + 1, :, g_ * 512:(g_ + 1) * 512], in_=qkst[s_][64:128, :]))(s3, h0, dst, g),
            reads=(qkstB[s3],))

    qunits = []
    for u in range(NU + 2):
        def qu(u=u):
            if u < NU:
                mainmm(u)
            if u >= 1 and u - 1 < NU:
                ssqmm(u - 1)
            if u >= 2 and u - 2 < NU:
                bcmm(u - 2)
        qunits.append(qu)
    fence = tuple(qkrawB + qksqB + sstdB + rinvB + qkstB + psQKB + psSQB + psBCB)
    return qunits, fence


def _v_section(nc, p, b, T, dr, cb, bufs, hand, qk, fence):
    NKT = T // 128
    cB = bufs["cB"]
    sfx = hand["sfx"]
    xnT, xnTgB = hand["xnT"], hand["xnTgB"]
    qkvb, qkvbB = qk["qkvb"], qk["qkvbB"]
    with ExitStack() as e3v:
        vstg = [e3v.enter_context(nc.sbuf_tensor("vstg%d%s" % (i, sfx), [128, 16 * 520], BF16))
                for i in range(2)]
        vstgB = [[Buf() for _ in range(8)] for _ in range(2)]
        psV = [e3v.enter_context(nc.psum_tensor("psV%d%s" % (i, sfx), [128, D], F32)) for i in range(2)]
        psVB = [Buf() for _ in range(2)]
        _vp(nc, "phAv" + sfx)
        for w in range(2):
            p.op("V", (lambda w_: lambda eng: eng.memset(
                ap2(vstg[w_], 64, [[16 * 520, 128], [65, 128]]), 1.0))(w),
                writes=tuple(vstgB[w]) + fence)
        for qt in range(NKT):
            i2 = qt % 2
            w = (qt // 8) % 2
            sl = qt % 8
            blds = []
            for n in (0, 512):
                for c in range(CC):
                    blds.append((lambda c_, n_, q_, i_: lambda eng: eng.matmul(
                        psV[i_][:, n_:n_ + 512], xnT[c_][:, q_ * 128:(q_ + 1) * 128],
                        qkvb[:, c_ * 3072 + 2048 + n_:c_ * 3072 + 2048 + n_ + 512],
                        start=(c_ == 0), stop=False))(c, n, qt, i2))
                blds.append((lambda n_, i_: lambda eng: eng.matmul(
                    psV[i_][:, n_:n_ + 512], cb["onesb"][:, :],
                    cb["bvrow"][0:1, n_:n_ + 512],
                    start=False, stop=True))(n, i2))
            p.opg("T", blds, reads=(xnTgB[qt // 4], qkvbB, cB),
                  writes=(psVB[i2],) + (fence if qt < 2 else ()))
            p.op("V", (lambda i_, w_, s_: lambda eng: eng.tensor_copy(
                ap2(vstg[w_], s_ * 65, [[16 * 520, 128], [520, 16], [1, 64]]),
                psV[i_][:, :]))(i2, w, sl),
                reads=(psVB[i2],), writes=(vstgB[w][sl],))
            if sl == 7 or qt == NKT - 1:
                nsl = sl + 1
                kt0 = qt - (nsl - 1)
                for h in range(NH):
                    p.dma("Y" if h % 2 == 0 else "S",
                          (lambda h_, k0, ns, w_: lambda eng: eng.dma_start(
                              out=ap2(dr["v_d"],
                                      ((b * NH + h_) * 128) * (NKT * 65) + k0 * 65,
                                      [[NKT * 65, 128], [1, ns * 65]]),
                              in_=ap2(vstg[w_], h_ * 520,
                                      [[16 * 520, 128], [1, ns * 65]])))(h, kt0, nsl, w),
                          reads=tuple(vstgB[w][0:nsl]))


def _merged(nc, p, b, T, dr, cb, bufs, hand, w1tb, w1B, dbg, dbgs):
    NKT = T // 128
    SW = min(1024, T)
    NSW = T // SW
    G = 256
    NG = T // G
    cB, mB = bufs["cB"], bufs["mB"]
    sfx = hand["sfx"]
    v1, v1B = hand["v1"], hand["v1B"]
    v2q, v2qB = hand["v2q"], hand["v2qB"]
    oddb, oddbB = hand["oddb"], hand["oddbB"]
    exM = ExitStack()
    w2tb = exM.enter_context(nc.sbuf_tensor("w2tb" + sfx, [128, 32768], BF16, side="right"))
    w2B = Buf()
    if b == 0:
        for c in range(0, CC, 2):
            p.dma("G", (lambda c_: lambda eng: eng.dma_start(
                out=dr["w1c"][:, c_ * HID:(c_ + 2) * HID],
                in_=w1tb[:, c_ * HID:(c_ + 2) * HID]))(c), reads=(w1B,))
    with ExitStack() as em:
        eme = em.enter_context
        xng = [eme(nc.sbuf_tensor("xng%d%s" % (i, sfx), [128, CC * G], BF16)) for i in range(2)]
        xngB = [Buf() for _ in range(2)]
        hT = eme(nc.sbuf_tensor("hT" + sfx, [128, 32 * G], BF16))
        hTgB = [Buf() for _ in range(4)]
        s1s = eme(nc.sbuf_tensor("s1s" + sfx, [4, G], BF16))
        s1sB = Buf()
        ttsq = eme(nc.sbuf_tensor("ttsq" + sfx, [128, G], BF16))
        ttsqB = Buf()
        p.op("V", lambda eng: eng.memset(ttsq[:, :], 0.0), writes=(ttsqB,))
        fstb = eme(nc.sbuf_tensor("fstb" + sfx, [128, D], F32))
        fstbB = Buf()
        qth = [eme(nc.sbuf_tensor("qth%d%s" % (i, sfx), [64, T], BF16)) for i in range(2)]
        qthB = [Buf() for _ in range(2)]
        kth = [eme(nc.sbuf_tensor("kth%d%s" % (i, sfx), [64, T], BF16)) for i in range(2)]
        kthB = [Buf() for _ in range(2)]
        vh = [eme(nc.sbuf_tensor("vh%d%s" % (i, sfx), [128, NKT * 65], BF16)) for i in range(2)]
        vhB = [Buf() for _ in range(2)]
        PT = [eme(nc.sbuf_tensor("PT%d%s" % (i, sfx), [128, SW], BF16)) for i in range(3)]
        PTB = [Buf() for _ in range(3)]
        aost = eme(nc.sbuf_tensor("aost" + sfx, [64, SW], BF16))
        aostB = Buf()
        lst = eme(nc.sbuf_tensor("lst" + sfx, [1, SW], F32))
        lstB = Buf()
        psA = eme(nc.psum_tensor("psA" + sfx, [128, G], F32))
        psAB = Buf()
        psF2 = eme(nc.psum_tensor("psF2" + sfx, [128, 512], F32))
        psF2B = Buf()
        psS = [eme(nc.psum_tensor("psSc%d%s" % (i, sfx), [128, SW], F32)) for i in range(2)]
        psSB = [Buf() for _ in range(2)]
        psPV = eme(nc.psum_tensor("psPV" + sfx, [65, SW], F32))
        psPVB = Buf()
        _vp(nc, "merged" + sfx)

        ffn_units = []

        def ffn_group(g):
            gs = g * G
            us = []
            if g + 1 < NG:
                us.append((lambda g_: lambda: p.dma("G", (lambda gg: lambda eng: eng.dma_start(
                    out=xng[gg % 2][:, :],
                    in_=ap2(dr["xn_d"], ((b * NG + gg) * 128) * (CC * G),
                            [[CC * G, 128], [1, CC * G]])))(g_),
                    writes=(xngB[(g_) % 2],)))(g + 1))
            us.append((lambda g_: lambda: (
                p.opg("T", [(lambda c_, i_: lambda eng: eng.matmul(
                    psA[0:4, :], oddb[:, c_ * 8:c_ * 8 + 4],
                    xng[i_][:, c_ * G:(c_ + 1) * G],
                    start=(c_ == 0), stop=(c_ == CC - 1)))(c, g_ % 2)
                    for c in range(CC)],
                    reads=(xngB[g_ % 2], oddbB), writes=(psAB,)),
                p.op("V", lambda eng: eng.tensor_scalar(
                    s1s[:, :], psA[0:4, :], cb["scolb"][0:4, 0:1], None, op0=ALU.mult),
                    reads=(psAB, cB), writes=(s1sB,))))(g))
            for fb in range(32):
                def u_f1(f_, g_):
                    def run():
                        blds = [(lambda c_, ff, i_: lambda eng: eng.matmul(
                            psA[:, :], w1tb[:, c_ * HID + ff * 128:c_ * HID + (ff + 1) * 128],
                            xng[i_][:, c_ * G:(c_ + 1) * G],
                            start=(c_ == 0), stop=False))(c, f_, g_ % 2)
                            for c in range(CC)]
                        blds.append((lambda ff: lambda eng: eng.matmul(
                            psA[:, :], v1[:, ff * 128:(ff + 1) * 128], s1s[:, :],
                            start=False, stop=True))(f_))
                        p.opg("T", blds, reads=(w1B, xngB[g_ % 2], v1B, s1sB), writes=(psAB,))
                        p.op("V", (lambda ff: lambda eng: eng.tensor_scalar(
                            hT[:, ff * G:(ff + 1) * G], psA[:, :],
                            cb["b1cb"][:, ff:ff + 1], None, op0=ALU.add))(f_),
                            reads=(psAB, cB), writes=(hTgB[f_ // 8],))
                    return run
                us.append(u_f1(fb, g))
            tail = []

            def u_gelu():
                for bb in range(4):
                    p.op("S", (lambda b_: lambda eng: eng.activation(
                        hT[:, b_ * 8 * G:(b_ + 1) * 8 * G],
                        hT[:, b_ * 8 * G:(b_ + 1) * 8 * G], AF.Gelu))(bb),
                        reads=(hTgB[bb],), writes=(hTgB[bb],))
            tail.append(u_gelu)
            for bp in range(4):
                def u_t(b_, g_):
                    def run():
                        p.opg("T", [(lambda bb, j_: lambda eng: eng.matmul(
                            psA[0:4, :], oddb[:, j_ * 8 + 4:j_ * 8 + 8],
                            hT[:, (bb * 8 + j_) * G:(bb * 8 + j_ + 1) * G],
                            start=(j_ == 0), stop=(j_ == CC - 1)))(b_, jc)
                            for jc in range(CC)],
                            reads=(oddbB, hTgB[b_]), writes=(psAB,))
                        p.op("V", (lambda bb: lambda eng: eng.tensor_scalar(
                            ttsq[32 * bb:32 * bb + 4, :], psA[0:4, :], cb["scolb"][0:4, 0:1],
                            None, op0=ALU.mult))(b_),
                            reads=(psAB, cB), writes=(ttsqB,))
                    return run
                tail.append(u_t(bp, g))
            for qtl in range(G // 128):
                for n in (0, 512):
                    def u_f2(q_, n_, g_):
                        def run():
                            qoff = q_ * 128
                            blds = []
                            for hc in range(32):
                                blds.append((lambda h_: lambda eng: eng.matmul(
                                    psF2[:, :],
                                    hT[:, h_ * G + qoff:h_ * G + qoff + 128],
                                    w2tb[:, h_ * D + n_:h_ * D + n_ + 512],
                                    start=(h_ == 0), stop=False))(hc))
                            blds.append(lambda eng: eng.matmul(
                                psF2[:, :], ttsq[:, qoff:qoff + 128],
                                v2q[:, n_:n_ + 512], start=False, stop=False))
                            blds.append(lambda eng: eng.matmul(
                                psF2[:, :], cb["onesb"][:, :],
                                cb["b2row"][0:1, n_:n_ + 512],
                                start=False, stop=True))
                            p.opg("T", blds,
                                  reads=tuple(hTgB) + (w2B, ttsqB, cB, v2qB),
                                  writes=(psF2B,))
                            p.op("V", lambda eng: eng.tensor_copy(
                                fstb[:, n_:n_ + 512], psF2[:, :]),
                                reads=(psF2B,), writes=(fstbB,))
                            if n_ == 512:
                                p.dma("S", (lambda t0: lambda eng: eng.dma_start(
                                    out=dr["ffnd"][b, t0:t0 + 128, :], in_=fstb[:, :]))(g_ * G + qoff),
                                    reads=(fstbB,))
                        return run
                    tail.append(u_f2(qtl, n, g))
            return us, tail

        pending_tail = []
        for g in range(NG):
            us, tail = ffn_group(g)
            ffn_units.extend(pending_tail)
            ffn_units.extend(us)
            pending_tail = tail
        ffn_units.extend(pending_tail)
        total_f = len(ffn_units)
        total_kt = NH * NSW * NKT
        state = {"fe": 0, "ke": 0}

        def pump():
            while (state["fe"] < total_f and
                   state["fe"] * total_kt <= state["ke"] * total_f):
                ffn_units[state["fe"]]()
                state["fe"] += 1

        def stream_head(h_):
            sl = h_ % 2
            p.dma("Y", (lambda hh, s_: lambda eng: eng.dma_start(
                out=kth[s_][:, :], in_=dr["k_d"][b, hh, :, :]))(h_, sl),
                writes=(kthB[sl],))
            p.dma("Y", (lambda hh, s_: lambda eng: eng.dma_start(
                out=qth[s_][:, :], in_=dr["q_d"][b, hh, :, :]))(h_, sl),
                writes=(qthB[sl],))
            p.dma("S", (lambda hh, s_: lambda eng: eng.dma_start(
                out=vh[s_][:, :],
                in_=ap2(dr["v_d"], ((b * NH + hh) * 128) * (NKT * 65),
                        [[NKT * 65, 128], [1, NKT * 65]])))(h_, sl),
                writes=(vhB[sl],))

        p.dma("G", lambda eng: eng.dma_start(
            out=xng[0][:, :],
            in_=ap2(dr["xn_d"], (b * NG) * 128 * (CC * G),
                    [[CC * G, 128], [1, CC * G]])), writes=(xngB[0],))
        stream_head(0)
        stream_head(1)
        if b == 0:
            for hq in range(8):
                p.dma("G", (lambda h_: lambda eng: eng.dma_start(
                    out=w2tb[:, h_ * 4 * D:(h_ + 1) * 4 * D],
                    in_=ap2(dr["w2t"], h_ * 512 * D,
                            [[D, 128], [128 * D, 4], [1, D]])))(hq), writes=(w2B,))
        else:
            for hq in range(0, 8, 2):
                p.dma("G", (lambda h_: lambda eng: eng.dma_start(
                    out=w2tb[:, h_ * 4096:(h_ + 2) * 4096],
                    in_=dr["w2c"][:, h_ * 4096:(h_ + 2) * 4096]))(hq), writes=(w2B,))
        if b == 0:
            for hq in range(0, 8, 2):
                p.dma("G", (lambda h_: lambda eng: eng.dma_start(
                    out=dr["w2c"][:, h_ * 4096:(h_ + 2) * 4096],
                    in_=w2tb[:, h_ * 4096:(h_ + 2) * 4096]))(hq), reads=(w2B,))

        for hh in range(NH):
            sl = hh % 2
            for sw in range(NSW):
                q0 = sw * SW

                def pv(kt):
                    i3 = kt % 3
                    edge = kt == 0 or kt == NKT - 1
                    p.opg("T", [(lambda k_, n_, i_, s_: lambda eng: eng.matmul(
                        psPV[:, n_:n_ + 512],
                        vh[s_][:, k_ * 65:(k_ + 1) * 65],
                        PT[i_][:, n_:n_ + 512],
                        start=(k_ == 0), stop=(k_ == NKT - 1)))(kt, n, i3, sl)
                        for n in range(0, SW, 512)],
                        reads=(PTB[i3], vhB[sl]),
                        writes=(psPVB,) if edge else ())

                for kt in range(NKT):
                    i2 = kt % 2
                    p.opg("T", [(lambda k_, n_, i_, s_, q_: lambda eng: eng.matmul(
                        psS[i_][:, n_:n_ + 512],
                        kth[s_][:, k_ * 128:(k_ + 1) * 128],
                        qth[s_][:, q_ + n_:q_ + n_ + 512],
                        start=True, stop=True))(kt, n, i2, sl, q0)
                        for n in range(0, SW, 512)],
                        reads=(kthB[sl], qthB[sl]), writes=(psSB[i2],))
                    p.op("S", (lambda i_, i3_: lambda eng: eng.activation(
                        PT[i3_][:, :], psS[i_][:, :], AF.Exp,
                        bias=cb["mbias"][:, 0:1], scale=0.125))(i2, kt % 3),
                        reads=(psSB[i2], mB), writes=(PTB[kt % 3],))
                    if kt > 1:
                        pv(kt - 2)
                    state["ke"] += 1
                    pump()
                pv(NKT - 2)
                pv(NKT - 1)
                p.op("V", lambda eng: eng.tensor_copy(aost[:, :], psPV[0:64, :]),
                     reads=(psPVB,), writes=(aostB,))
                p.op("V", lambda eng: eng.tensor_copy(lst[:, :], psPV[64:65, :]),
                     reads=(psPVB,), writes=(lstB,))
                p.dma("Y", (lambda hh_, q_: lambda eng: eng.dma_start(
                    out=dr["ao_d"][b, hh_ >> 1, 64 * (hh_ & 1):64 * (hh_ & 1) + 64, q_:q_ + SW],
                    in_=aost[:, :]))(hh, q0),
                    reads=(aostB,))
                p.dma("S", (lambda hh_, q_: lambda eng: eng.dma_start(
                    out=dr["l_d"][b, hh_, q_:q_ + SW], in_=lst[:, :]))(hh, q0),
                    reads=(lstB,))
            if hh + 2 < NH:
                stream_head(hh + 2)
        while state["fe"] < total_f:
            ffn_units[state["fe"]]()
            state["fe"] += 1
        if dbg:
            p.dma("Y", lambda eng: eng.dma_start(out=dbgs["hT"][:, :], in_=hT[:, :]),
                  reads=tuple(hTgB))
    exM.close()


def _proj_units(nc, p, b, T, dr, cb, bufs, ep, overlap=False):
    NTI = T // 128
    SW = min(1024, T)
    NP = NH // 2
    cB = bufs["cB"]
    sfx = "_p%d" % b
    epe = ep.enter_context
    wprojb = epe(nc.sbuf_tensor("wprojb" + sfx, [128, 8192], BF16))
    wpB = Buf()
    aoTn = epe(nc.sbuf_tensor("aoTn" + sfx, [128, 8 * T], BF16))
    aoTnB = Buf()
    aorw = [epe(nc.sbuf_tensor("aorw%d%s" % (i, sfx), [128, T], BF16)) for i in range(2)]
    aorwB = [Buf() for _ in range(2)]
    lpr = epe(nc.sbuf_tensor("lpr" + sfx, [NH, T], F32))
    lprB = Buf()
    linv16 = epe(nc.sbuf_tensor("linv16" + sfx, [NH, T], BF16))
    linv16B = Buf()
    leo = [epe(nc.sbuf_tensor("leo%d%s" % (i, sfx), [1, 2 * T], BF16)) for i in range(2)]
    leoB = [Buf() for _ in range(2)]
    xres = [epe(nc.sbuf_tensor("xres%d%s" % (i, sfx), [128, D], F32)) for i in range(3)]
    xresB = [Buf() for _ in range(3)]
    fres = [epe(nc.sbuf_tensor("fres%d%s" % (i, sfx), [128, D], F32)) for i in range(3)]
    fresB = [Buf() for _ in range(3)]
    yt = [epe(nc.sbuf_tensor("yt%d%s" % (i, sfx), [128, D], F32)) for i in range(2)]
    ytB = [Buf() for _ in range(2)]
    psBC2 = [epe(nc.psum_tensor("psBC2%d%s" % (i, sfx), [128, SW], F32)) for i in range(2)]
    psBC2B = [Buf() for _ in range(2)]
    psPJ = [epe(nc.psum_tensor("psPJ%d%s" % (i, sfx), [128, 512 if overlap else D], F32))
            for i in range(2)]
    psPJB = [Buf() for _ in range(2)]
    _vp(nc, "proj" + sfx)

    def fetch_res(q_):
        i3_ = q_ % 3
        xq = "S" if q_ % 2 == 1 else "Y"
        fq = "S" if (overlap or q_ % 2 == 0) else "G"
        p.dma(xq, (lambda qq, ii: lambda eng: eng.dma_start(
            out=xres[ii][:, :], in_=dr["xd"][b, qq * 128:(qq + 1) * 128, :]))(q_, i3_),
            writes=(xresB[i3_],))
        p.dma(fq, (lambda qq, ii: lambda eng: eng.dma_start(
            out=fres[ii][:, :], in_=dr["ffnd"][b, qq * 128:(qq + 1) * 128, :]))(q_, i3_),
            writes=(fresB[i3_],))

    units = []

    def wpload():
        for c in range(CC):
            p.dma("G", (lambda c_: lambda eng: eng.dma_start(
                out=wprojb[:, c_ * D:(c_ + 1) * D],
                in_=dr["wproj"][c_ * 128:(c_ + 1) * 128, :]))(c), writes=(wpB,))

    def u0():
        if not overlap:
            wpload()
        for q_ in range(min(3, NTI)):
            fetch_res(q_)
        p.dma("S", lambda eng: eng.dma_start(out=lpr[:, :], in_=dr["l_d"][b, :, :]),
              writes=(lprB,))
        p.op("S", lambda eng: eng.activation(lpr[:, :], lpr[:, :], AF.Ln),
             reads=(lprB,), writes=(lprB,))
        p.op("S", lambda eng: eng.activation(linv16[:, :], lpr[:, :], AF.Exp, scale=-1.0),
             reads=(lprB,), writes=(linv16B,))
    units.append(u0)

    def u_pg(pg):
        i2 = pg % 2
        p.dma("Y", (lambda p_, i_: lambda eng: eng.dma_start(
            out=aorw[i_][:, :],
            in_=ap2(dr["ao_d"], ((b * NP + p_) * 128) * T,
                    [[T, 128], [1, T]])))(pg, i2),
            writes=(aorwB[i2],))
        for r in range(2):
            p.dma("S", (lambda p_, i_, r_: lambda eng: eng.dma_start(
                out=leo[i_][:, r_ * T:(r_ + 1) * T],
                in_=linv16[2 * p_ + r_:2 * p_ + r_ + 1, :]))(pg, i2, r),
                reads=(linv16B,), writes=(leoB[i2],))
        for q0 in range(0, T, SW):
            p.opg("T", [(lambda hf, n_, q_, i_, j_: lambda eng: eng.matmul(
                psBC2[j_][hf:hf + 64, n_:n_ + 512], cb["onesb"][0:1, 0:64],
                leo[i_][0:1, (hf // 64) * T + q_ + n_:(hf // 64) * T + q_ + n_ + 512],
                start=True, stop=True))(half, n, q0, i2, pg % 2)
                for half in (0, 64) for n in range(0, SW, 512)],
                reads=(leoB[i2], cB), writes=(psBC2B[pg % 2],))
            p.op("V", (lambda p_, i_, q_, j_: lambda eng: eng.tensor_tensor(
                aoTn[:, p_ * T + q_:p_ * T + q_ + SW],
                aorw[i_][:, q_:q_ + SW],
                psBC2[j_][:, :], op=ALU.mult))(pg, i2, q0, pg % 2),
                reads=(psBC2B[pg % 2], aorwB[i2]), writes=(aoTnB,))
    for pg in range(NP):
        units.append((lambda p_: lambda: u_pg(p_))(pg))
        if overlap and pg == 2:
            units.append(wpload)

    def u_qt(qt):
        i2 = qt % 2
        i3r = qt % 3
        p.op("V", (lambda i_, j_: lambda eng: eng.tensor_add(
            yt[i_][:, :], xres[j_][:, :], fres[j_][:, :]))(i2, i3r),
            reads=(xresB[i3r], fresB[i3r]), writes=(ytB[i2],))
        if overlap:
            for ih, n in enumerate((0, 512)):
                blds = []
                for c in range(CC):
                    blds.append((lambda c_, n_, q_, i_: lambda eng: eng.matmul(
                        psPJ[i_][:, :], aoTn[:, c_ * T + q_ * 128:c_ * T + (q_ + 1) * 128],
                        wprojb[:, c_ * D + n_:c_ * D + n_ + 512],
                        start=(c_ == 0), stop=False))(c, n, qt, ih))
                blds.append((lambda n_, i_: lambda eng: eng.matmul(
                    psPJ[i_][:, :], cb["onesb"][:, :], cb["bprow"][0:1, n_:n_ + 512],
                    start=False, stop=True))(n, ih))
                p.opg("T", blds, reads=(wpB, cB, aoTnB), writes=(psPJB[ih],))
                p.op("V", (lambda i_, n_, j_: lambda eng: eng.tensor_add(
                    yt[i_][:, n_:n_ + 512], yt[i_][:, n_:n_ + 512], psPJ[j_][:, :]))(i2, n, ih),
                    reads=(psPJB[ih], ytB[i2]), writes=(ytB[i2],))
        else:
            blds = []
            for n in (0, 512):
                for c in range(CC):
                    blds.append((lambda c_, n_, q_, i_: lambda eng: eng.matmul(
                        psPJ[i_][:, n_:n_ + 512], aoTn[:, c_ * T + q_ * 128:c_ * T + (q_ + 1) * 128],
                        wprojb[:, c_ * D + n_:c_ * D + n_ + 512],
                        start=(c_ == 0), stop=False))(c, n, qt, i2))
                blds.append((lambda n_, i_: lambda eng: eng.matmul(
                    psPJ[i_][:, n_:n_ + 512], cb["onesb"][:, :], cb["bprow"][0:1, n_:n_ + 512],
                    start=False, stop=True))(n, i2))
            p.opg("T", blds, reads=(wpB, cB, aoTnB), writes=(psPJB[i2],))
            p.op("V", (lambda i_: lambda eng: eng.tensor_add(
                yt[i_][:, :], yt[i_][:, :], psPJ[i_][:, :]))(i2),
                reads=(psPJB[i2], ytB[i2]), writes=(ytB[i2],))
        p.dma("Y" if qt % 2 == 0 else "S", (lambda q_, i_: lambda eng: eng.dma_start(
            out=dr["yd"][b, q_ * 128:(q_ + 1) * 128, :], in_=yt[i_][:, :]))(qt, i2),
            reads=(ytB[i2],))
        if qt + 3 < NTI:
            fetch_res(qt + 3)
    for qt in range(NTI):
        units.append((lambda q_: lambda: u_qt(q_))(qt))
    return units


def _host_consts():
    bf = ml_dtypes.bfloat16
    ident = np.eye(128, dtype=bf)
    onesb = np.ones((1, 128), dtype=bf)
    sel2 = np.zeros((2, 128), dtype=bf)
    sel2[0, 0:64] = 1
    sel2[1, 64:128] = 1
    sel128 = np.zeros((128, 2), dtype=bf)
    sel128[0:64, 0] = 1
    sel128[64:128, 1] = 1
    return ident, onesb, sel2, sel128


def make_in_maps(inputs, NB, T, n_cores):
    x = np.ascontiguousarray(np.asarray(inputs["L0_attn_x"], dtype=np.float32))
    wqkv = np.ascontiguousarray(np.asarray(inputs["L0_attn_W_qkv"], dtype=np.float32))
    bqkv = np.ascontiguousarray(np.asarray(inputs["L0_attn_b_qkv"], dtype=np.float32))
    wproj = np.ascontiguousarray(np.asarray(inputs["L0_attn_W_proj"], dtype=np.float32))
    bproj = np.ascontiguousarray(np.asarray(inputs["L0_attn_b_proj"], dtype=np.float32))
    qn = np.ascontiguousarray(np.asarray(inputs["L0_attn_qn_w"], dtype=np.float32).reshape(DH, 1))
    kn = np.ascontiguousarray(np.asarray(inputs["L0_attn_kn_w"], dtype=np.float32).reshape(DH, 1))
    w1 = np.asarray(inputs["L0_ffn_W1"], dtype=np.float32)
    b1 = np.asarray(inputs["L0_ffn_b1"], dtype=np.float32)
    w2 = np.asarray(inputs["L0_ffn_W2"], dtype=np.float32)
    b2 = np.ascontiguousarray(np.asarray(inputs["L0_ffn_b2"], dtype=np.float32))
    scale = np.asarray(inputs["L0_ffn_scale"], dtype=np.float32)

    w1t = np.ascontiguousarray(w1.T)
    w2t = np.ascontiguousarray(w2.T)
    b1c = np.ascontiguousarray(b1.reshape(32, 128).T)
    bqkvc = np.ascontiguousarray(bqkv[:2048].reshape(16, 128).T)
    scol = np.ascontiguousarray(np.broadcast_to(scale.reshape(1, 1), (128, 1)))
    ident, onesb, sel2, sel128 = _host_consts()

    xs = x.reshape(n_cores, NB, T, D)
    common = dict(wqkv=wqkv, bqkv=bqkv, bqkvc=bqkvc, wproj=wproj, bproj=bproj,
                  qn=qn, kn=kn, w1t=w1t, b1c=b1c, w2t=w2t, b2=b2, scol=scol,
                  identd=ident, onesbd=onesb, sel2d=sel2,
                  sel128d=sel128)
    return [dict(common, x=np.ascontiguousarray(xs[i])) for i in range(n_cores)]


def kernel(**inputs):
    NB, T, NCORES = 2, 2048, 8
    nc = build_program(NB, T, dbg=False)
    in_maps = make_in_maps(inputs, NB, T, NCORES)
    res = run_bass_kernel_spmd(nc, in_maps, core_ids=list(range(NCORES)))
    y = np.concatenate([np.asarray(r["y"]) for r in res.results], axis=0)
    return np.ascontiguousarray(y.reshape(16, 2048, 1024)).astype(np.float32)
